# version 16
# baseline (speedup 1.0000x reference)
import numpy as np
from contextlib import ExitStack

import concourse.bass as bass
import concourse.mybir as mybir
from concourse.bass_utils import run_bass_kernel_spmd

F32 = mybir.dt.float32
F32R = mybir.dt.float32r
AF = mybir.ActivationFunctionType
ALU = mybir.AluOpType
AX = mybir.AxisListType

B, S, DM = 4, 4096, 1024
NH, DH, HP = 16, 64, 8
SOWN = 2048
NT_FULL, NT_OWN = S // 128, SOWN // 128
NEXP, ESZ, UDIM = 64, 128, 8192
NUB = 16
EPS = 1e-5
ATTN_SCALE = 0.125

DEBUG_TAPS = False


class Sem:
    def __init__(self, h):
        self.h = h
        self.n = 0

    def inc(self, k=1):
        self.n += k
        return self.n


class Emitter:
    ENGINES = ("sync", "tensor", "vector", "scalar", "gpsimd")

    def __init__(self, block):
        self.block = block
        self.pend = {k: [] for k in self.ENGINES}

    def on(self, eng, fn):
        self.pend[eng].append(fn)

    def flush(self):
        for eng in self.ENGINES:
            fns = self.pend[eng]
            if not fns:
                continue
            self.pend[eng] = []
            getattr(self.block, eng)(lambda e, fns=fns: [f(e) for f in fns])


def build_program():
    nc = bass.Bass("TRN2", target_bir_lowering=False)
    dt = lambda name, shape, kind: nc.dram_tensor(name, shape, F32, kind=kind).ap()

    x_full = dt("x_full", [S, DM], "ExternalInput")
    x_own = dt("x_own", [SOWN, DM], "ExternalInput")
    wq = dt("wq", [DM, DM], "ExternalInput")
    wk = dt("wk", [DM, DM], "ExternalInput")
    wv = dt("wv", [DM, DM], "ExternalInput")
    wo = dt("wo", [DM, DM], "ExternalInput")
    bqk = dt("bqk", [128, 16], "ExternalInput")
    bv_rep = dt("bv_rep", [128, DM], "ExternalInput")
    bo_rep = dt("bo_rep", [128, DM], "ExternalInput")
    selT_d = dt("selT", [DM, NEXP], "ExternalInput")
    keys_l = dt("keys_l", [DM, UDIM], "ExternalInput")
    values_l = dt("values_l", [UDIM, DM], "ExternalInput")
    cos_k = dt("cos_k", [128, S], "ExternalInput")
    sin_k = dt("sin_k", [128, S], "ExternalInput")
    cos_q = dt("cos_q", [128, SOWN], "ExternalInput")
    sin_q = dt("sin_q", [128, SOWN], "ExternalInput")
    ident = dt("ident", [128, 128], "ExternalInput")
    halfsel = dt("halfsel", [2, 128], "ExternalInput")
    bmask = dt("bmask", [2, 128, 128], "ExternalInput")

    out = dt("out", [SOWN, DM], "ExternalOutput")
    skind = "ExternalOutput" if DEBUG_TAPS else "Internal"
    xnT_full_h = dt("xnT_full_h", [DM, S], skind)
    xnT_own_h = dt("xnT_own_h", [DM, SOWN], skind)
    znT_h = dt("znT_h", [DM, SOWN], skind)
    l_h = dt("l_h", [2, HP * SOWN], skind)
    gates_h = dt("gates_h", [128, NT_OWN * 64], skind) if DEBUG_TAPS else None
    acc3_h = dt("acc3_h", [SOWN, DM], skind) if DEBUG_TAPS else None

    ctx = ExitStack()
    with ctx:
        sb = lambda name, shape: ctx.enter_context(nc.sbuf_tensor(name, shape, F32))
        ident_sb = sb("ident_sb", [128, 128])
        ones1_sb = sb("ones1_sb", [1, 128])
        gates_sb = sb("gates_sb", [128, NT_OWN * 64])

        bank = [ctx.enter_context(nc.psum_tensor(f"bank{i}", [128, 512], F32))
                for i in range(8)]

        s_in = Sem(ctx.enter_context(nc.semaphore("s_in")))
        s_out = Sem(ctx.enter_context(nc.semaphore("s_out")))
        s_t = Sem(ctx.enter_context(nc.semaphore("s_t")))
        s_v = Sem(ctx.enter_context(nc.semaphore("s_v")))
        s_s = Sem(ctx.enter_context(nc.semaphore("s_s")))

        block = ctx.enter_context(nc.Block())
        em = Emitter(block)

        def p0(e):
            e.dma_start(out=ident_sb[:], in_=ident[:]).then_inc(s_in.h, 16)
        em.on("sync", p0)
        consts_at = s_in.inc(16)

        def p0v(e):
            e.memset(ones1_sb[:], 1.0).then_inc(s_v.h, 1)
        em.on("vector", p0v)
        s_v.inc()

        with ExitStack() as p1:
            sb1 = lambda n, shp: p1.enter_context(nc.sbuf_tensor(n, shp, F32))
            NP1 = 4
            xt = [sb1(f"p1_x{i}", [128, DM]) for i in range(NP1)]
            xsq = [sb1(f"p1_xsq{i}", [128, DM]) for i in range(2)]
            stat = [sb1(f"p1_stat{i}", [128, 8]) for i in range(2)]
            xnt = [sb1(f"p1_xn{i}", [128, DM]) for i in range(NP1)]
            xnTt = [sb1(f"p1_xnT{i}", [128, DM]) for i in range(NP1)]
            logit_sb = sb1("p1_logit", [128, NT_OWN * 64])
            selT_sb = sb1("p1_selT", [128, 8 * 64])
            max8 = sb1("p1_max8", [128, 8])
            gmask = sb1("p1_gmask", [128, 64])

            def ld_selT(e):
                e.dma_start(out=selT_sb.ap().rearrange("p (c e) -> p c e", c=8),
                            in_=selT_d.rearrange("(c p) e -> p c e", p=128)).then_inc(s_in.h, 16)
            em.on("sync", ld_selT)
            selT_at = s_in.inc(16)

            def p1t0(e):
                e.wait_ge(s_in.h, consts_at)
                e.wait_ge(s_in.h, selT_at)
            em.on("tensor", p1t0)

            xt_free = [0] * NP1
            xn_free = [0] * NP1
            xnT_free = [0] * NP1
            tbank_free = 0
            tbank_free_prev = 0
            lbank_free = 0
            xsq_free = [0, 0]

            def p1_phase1(i, src_ap):
                nonlocal xsq_free
                slot = i % NP1

                def ld(e, slot=slot, src_ap=src_ap, lw=xt_free[slot]):
                    if lw:
                        e.wait_ge(s_v.h, lw)
                    e.dma_start(out=xt[slot][:], in_=src_ap).then_inc(s_in.h, 16)
                em.on("sync", ld)
                x_at = s_in.inc(16)

                sq_slot = i % 2
                st_ = stat[i % 2]

                def sq(e, slot=slot, x_at=x_at, xw=xsq_free[sq_slot], sq_slot=sq_slot):
                    e.wait_ge(s_in.h, x_at)
                    if xw:
                        e.wait_ge(s_v.h, xw)
                    e.activation(out=xsq[sq_slot][:], in_=xt[slot][:], func=AF.Square).then_inc(s_s.h, 1)
                em.on("scalar", sq)
                sq_at = s_s.inc()

                def ln(e, slot=slot, x_at=x_at, sq_at=sq_at, st_=st_, sq_slot=sq_slot):
                    e.wait_ge(s_in.h, x_at)
                    e.reduce_sum(st_[:, 0:1], xt[slot][:], axis=AX.X)
                    e.wait_ge(s_s.h, sq_at)
                    e.reduce_sum(st_[:, 1:2], xsq[sq_slot][:], axis=AX.X)
                    e.drain()
                    e.tensor_scalar(st_[:, 2:3], st_[:, 0:1], 1.0 / DM, None, op0=ALU.mult)
                    e.drain()
                    e.tensor_scalar(st_[:, 3:4], st_[:, 2:3], st_[:, 2:3], None, op0=ALU.mult)
                    e.drain()
                    e.tensor_scalar(st_[:, 4:5], st_[:, 1:2], 1.0 / DM, st_[:, 3:4],
                                    op0=ALU.mult, op1=ALU.subtract)
                    e.drain()
                    e.tensor_scalar(st_[:, 5:6], st_[:, 4:5], EPS, None, op0=ALU.add)
                    e.drain()
                    e.reciprocal(st_[:, 6:7], st_[:, 5:6]).then_inc(s_v.h, 1)
                em.on("vector", ln)
                rv_at = s_v.inc()
                xsq_free[sq_slot] = rv_at

                def rsq(e, rv_at=rv_at, st_=st_):
                    e.wait_ge(s_v.h, rv_at)
                    e.activation(out=st_[:, 7:8], in_=st_[:, 6:7], func=AF.Sqrt).then_inc(s_s.h, 1)
                em.on("scalar", rsq)
                inv_at = s_s.inc()
                return (i, slot, inv_at)

            def p1_phase2(info, dst_h, own_idx):
                nonlocal tbank_free, lbank_free, tbank_free_prev
                i, slot, inv_at = info
                st_ = stat[i % 2]
                def norm(e, slot=slot, inv_at=inv_at, lw=xn_free[slot], st_=st_):
                    e.wait_ge(s_s.h, inv_at)
                    if lw:
                        e.wait_ge(s_t.h, lw)
                    e.tensor_scalar(xnt[slot][:], xt[slot][:], st_[:, 2:3], st_[:, 7:8],
                                    op0=ALU.subtract, op1=ALU.mult).then_inc(s_v.h, 1)
                em.on("vector", norm)
                xn_at = s_v.inc()
                xt_free[slot] = xn_at

                tb0 = 2 * (i % 2)

                def tps(e, slot=slot, xn_at=xn_at, bw=tbank_free, tb0=tb0):
                    e.wait_ge(s_v.h, xn_at)
                    if bw:
                        e.wait_ge(s_v.h, bw)
                    for c in range(8):
                        inst = e.transpose(bank[tb0 + c // 4][:, (c % 4) * 128:(c % 4) * 128 + 128],
                                           xnt[slot][:, c * 128:c * 128 + 128], ident_sb[:])
                        if c in (3, 7):
                            inst.then_inc(s_t.h, 1)
                em.on("tensor", tps)
                tp_half = [s_t.inc(), s_t.inc()]

                def ev(e, slot=slot, tp_half=tuple(tp_half), lw=xnT_free[slot], tb0=tb0):
                    if lw:
                        e.wait_ge(s_out.h, lw)
                    e.wait_ge(s_t.h, tp_half[0])
                    e.tensor_copy(xnTt[slot][:, 0:512], bank[tb0][:])
                    e.wait_ge(s_t.h, tp_half[1])
                    e.tensor_copy(xnTt[slot][:, 512:1024], bank[tb0 + 1][:]).then_inc(s_v.h, 1)
                em.on("vector", ev)
                xnT_at = s_v.inc()
                tbank_free, tbank_free_prev = tbank_free_prev, xnT_at

                lg_at = None
                if own_idx is not None:
                    def lgmm(e, slot=slot, xnT_at=xnT_at, lbw=lbank_free):
                        e.wait_ge(s_v.h, xnT_at)
                        if lbw:
                            e.wait_ge(s_s.h, lbw)
                        for c in range(8):
                            inst = e.matmul(bank[4][:, 0:64], xnTt[slot][:, c * 128:c * 128 + 128],
                                            selT_sb[:, c * 64:c * 64 + 64],
                                            start=(c == 0), stop=(c == 7))
                            if c == 7:
                                inst.then_inc(s_t.h, 1)
                    em.on("tensor", lgmm)
                    lg_at = s_t.inc()

                    def lgev(e, lg_at=lg_at, own_idx=own_idx):
                        e.wait_ge(s_t.h, lg_at)
                        e.activation(out=logit_sb[:, own_idx * 64:own_idx * 64 + 64],
                                     in_=bank[4][:, 0:64], func=AF.Copy, scale=1.0).then_inc(s_s.h, 1)
                    em.on("scalar", lgev)
                    lbank_free = s_s.inc()
                xn_free[slot] = lg_at if lg_at is not None else tp_half[1]

                def stv(e, slot=slot, xnT_at=xnT_at, dst_h=dst_h, i=i,
                        lg_at=lg_at):
                    e.wait_ge(s_v.h, xnT_at)
                    e.dma_start(out=dst_h.rearrange("(c p) t -> p c t", p=128)[:, :, i * 128:i * 128 + 128],
                                in_=xnTt[slot].ap().rearrange("p (c t) -> p c t", c=8)).then_inc(s_out.h, 16)
                em.on("gpsimd", stv)
                xnT_free[slot] = s_out.inc(16)

            p1_pend = None
            for i in range(NT_FULL + NT_OWN):
                if i < NT_FULL:
                    info = p1_phase1(i, x_full[i * 128:(i + 1) * 128, :])
                    nxt = (info, xnT_full_h, None)
                else:
                    j = i - NT_FULL
                    info = p1_phase1(NT_FULL + j, x_own[j * 128:(j + 1) * 128, :])
                    info = (j, info[1], info[2])
                    nxt = (info, xnT_own_h, j)
                if p1_pend is not None:
                    p1_phase2(*p1_pend)
                p1_pend = nxt
                if i % 4 == 3:
                    em.flush()
            p1_phase2(*p1_pend)
            p1_pend = None

            lg_done = s_s.n

            def sig(e, lg_done=lg_done):
                e.wait_ge(s_s.h, lg_done)
                e.activation(out=logit_sb[:], in_=logit_sb[:], func=AF.Sigmoid).then_inc(s_s.h, 1)
            em.on("scalar", sig)
            sig_at = s_s.inc()

            def gates(e, sig_at=sig_at):
                e.wait_ge(s_s.h, sig_at)
                for i in range(NT_OWN):
                    sl = slice(i * 64, i * 64 + 64)
                    e.max(max8[:], logit_sb[:, sl])
                    e.drain()
                    e.tensor_scalar(gmask[:], logit_sb[:, sl], max8[:, 3:4], None, op0=ALU.is_ge)
                    inst = e.tensor_tensor(gates_sb[:, sl], logit_sb[:, sl], gmask[:], op=ALU.mult)
                    if i == NT_OWN - 1:
                        inst.then_inc(s_v.h, 1)
            em.on("vector", gates)
            gates_at = s_v.inc()

            if DEBUG_TAPS:
                def st_gates(e, gates_at=gates_at):
                    e.wait_ge(s_v.h, gates_at)
                    e.dma_start(out=gates_h[:], in_=gates_sb[:]).then_inc(s_out.h, 16)
                em.on("gpsimd", st_gates)
                s_out.inc(16)
            em.flush()

        p1_stores = s_out.n

        with ExitStack() as p2:
            sb2 = lambda n, shp: p2.enter_context(nc.sbuf_tensor(n, shp, F32))
            m1_sb = sb2("m1_sb", [128, 128])
            m2_sb = sb2("m2_sb", [128, 128])
            bqk_sb = sb2("bqk_sb", [128, 16])
            bv_sb = sb2("bv_sb", [128, DM])
            cosk_sb = sb2("cosk", [128, S])
            sink_sb = sb2("sink", [128, S])
            cosq_sb = sb2("cosq", [128, SOWN])
            sinq_sb = sb2("sinq", [128, SOWN])
            stream = [sb2(f"p2_st{i}", [128, 8 * 512]) for i in range(2)]
            wq_sb = sb2("p2_wq", [128, 8 * 128])
            wk_sb = sb2("p2_wk", [128, 8 * 128])
            wv_sb = sb2("p2_wv", [128, 8 * 128])
            krope = sb2("p2_krope", [128, S])
            kflip = sb2("p2_kflip", [128, S])
            qrope = sb2("p2_qrope", [128, SOWN])
            qflip = sb2("p2_qflip", [128, SOWN])
            vhp = sb2("p2_v", [128, NT_FULL * 130])
            absm = sb2("p2_absm", [128, 2])
            mtmp = sb2("p2_mtmp", [1, 16])
            negM = sb2("p2_negM", [128, 2])
            sq2 = p2.enter_context(nc.sbuf_tensor("p2_sq2", [128, S + SOWN], mybir.dt.bfloat16))
            ones128b = p2.enter_context(nc.sbuf_tensor("p2_ones128b", [128, 1], mybir.dt.bfloat16))
            psb = [[sb2(f"p2_p{h}{gbuf}", [128, 512]) for gbuf in range(2)] for h in range(2)]
            znT_sl = [sb2(f"p2_znTsl{h}", [65, 512]) for h in range(2)]

            def ld_tabs(e, p1s=p1_stores, ga=gates_at, sa=sig_at):
                e.wait_ge(s_out.h, p1s)
                e.wait_ge(s_v.h, ga)
                e.wait_ge(s_s.h, sa)
                for dst, src in ((cosk_sb, cos_k), (sink_sb, sin_k),
                                 (cosq_sb, cos_q), (sinq_sb, sin_q),
                                 (m1_sb, bmask[0]), (m2_sb, bmask[1]),
                                 (bqk_sb, bqk), (bv_sb, bv_rep)):
                    e.dma_start(out=dst[:], in_=src if src is not cos_k else src[:]).then_inc(s_in.h, 16)
            em.on("sync", ld_tabs)
            tabs_at = s_in.inc(128)

            def p2init(e):
                e.memset(ones128b[:], 1.0)
                e.memset(vhp.ap().rearrange("p (t x) -> p t x", t=NT_FULL)[:, :, 64:65], 1.0)
                e.memset(vhp.ap().rearrange("p (t x) -> p t x", t=NT_FULL)[:, :, 129:130], 1.0).then_inc(s_v.h, 1)
            em.on("vector", p2init)
            vones_at = s_v.inc()

            stream_free = [0, 0]
            hp_prev_v = 0
            av_buf_last = {0: 0, 1: 0}
            bankpair_free = {0: 0, 1: 0}
            zfree_par = [0, 0]
            av_done_last = 0
            znT_free = 0

            proj_last_t = 0
            pend_div = None
            for hp in range(HP):
                def ldw(e, hp=hp, pw=proj_last_t):
                    if pw:
                        e.wait_ge(s_t.h, pw)
                    for dst, srcw in ((wq_sb, wq), (wk_sb, wk), (wv_sb, wv)):
                        e.dma_start(out=dst.ap().rearrange("p (c m) -> p c m", c=8),
                                    in_=srcw.rearrange("(c p) m -> p c m", p=128)[:, :, hp * 128:hp * 128 + 128],
                                    ).then_inc(s_in.h, 16)
                em.on("sync", ldw)
                w_at = s_in.inc(48)

                prev_evac = 0
                vband_free = [0, 0]
                flip_at = {}
                rope_at_sl = {}
                sq_at_sl = {}
                nred_prev = 0
                pend_nmm = []

                def emit_pend_div():
                    nonlocal pend_div
                    if pend_div is not None:
                        pend_div()
                        pend_div = None

                def emit_pend_nmm():
                    while len(pend_nmm) > 1:
                        pend_nmm.pop(0)()

                def drain_pend_nmm():
                    while pend_nmm:
                        pend_nmm.pop(0)()

                def emit_ropechain(si, buf, flp, costab, sintab, off, n):
                    nonlocal pend_nmm

                    def sqs(e, pd=prev_evac, buf=buf, si=si, off=off, n=n):
                        e.wait_ge(s_v.h, pd)
                        e.activation(out=sq2[:, si * 512:si * 512 + n], in_=buf[:, off:off + n],
                                     func=AF.Square).then_inc(s_s.h, 1)
                    em.on("scalar", sqs)
                    sq_at_sl[si] = s_s.inc()

                    def flips(e, pd=prev_evac, buf=buf, flp=flp, off=off, n=n):
                        e.wait_ge(s_v.h, pd)
                        for b0 in (0, 64):
                            e.dma_start(out=flp[b0:b0 + 32, off:off + n],
                                        in_=buf[b0 + 32:b0 + 64, off:off + n]).then_inc(s_out.h, 16)
                            e.dma_start(out=flp[b0 + 32:b0 + 64, off:off + n],
                                        in_=buf[b0:b0 + 32, off:off + n]).then_inc(s_out.h, 16)
                    em.on("gpsimd", flips)
                    flip_at[si] = s_out.inc(64)

                    def rope(e, fa=flip_at[si], sa=sq_at_sl[si], buf=buf, flp=flp,
                             costab=costab, sintab=sintab, off=off, n=n):
                        e.wait_ge(s_out.h, fa)
                        e.wait_ge(s_s.h, sa)
                        e.tensor_tensor(buf[:, off:off + n], buf[:, off:off + n],
                                        costab[:, off:off + n], op=ALU.mult)
                        e.tensor_tensor(flp[:, off:off + n], flp[:, off:off + n],
                                        sintab[:, off:off + n], op=ALU.mult)
                        e.tensor_tensor(buf[:, off:off + n], buf[:, off:off + n],
                                        flp[:, off:off + n], op=ALU.add).then_inc(s_v.h, 1)
                    em.on("vector", rope)
                    rope_at_sl[si] = s_v.inc()

                    def make_nmm(si=si, n=n, sa=sq_at_sl[si]):
                        def emit():
                            nonlocal nred_prev

                            def nmm(e, sa=sa, si=si, n=n, rp=nred_prev):
                                e.wait_ge(s_s.h, sa)
                                if rp:
                                    e.wait_ge(s_v.h, rp)
                                e.matmul(bank[3][0:1, 0:n], ones128b[:],
                                         sq2[:, si * 512:si * 512 + n],
                                         start=True, stop=True).then_inc(s_t.h, 1)
                            em.on("tensor", nmm)
                            nmm_at = s_t.inc()

                            def nred(e, na=nmm_at, si=si, n=n):
                                e.wait_ge(s_t.h, na)
                                e.tensor_reduce(mtmp[:, 2 + si:3 + si], bank[3][0:1, 0:n],
                                                axis=AX.X, op=ALU.max).then_inc(s_v.h, 1)
                            em.on("vector", nred)
                            nred_prev = s_v.inc()
                        return emit
                    pend_nmm.append(make_nmm())

                for sl_i in range(8):
                    ssl = sl_i % 2

                    def ldsl(e, ssl=ssl, sl_i=sl_i, lw=stream_free[ssl], hp=hp):
                        if hp == 0 and sl_i == 0:
                            e.wait_ge(s_out.h, p1_stores)
                        if lw:
                            e.wait_ge(s_t.h, lw)
                        e.dma_start(
                            out=stream[ssl].ap().rearrange("p (c t) -> p c t", c=8),
                            in_=xnT_full_h.rearrange("(c p) t -> p c t", p=128)[:, :, sl_i * 512:sl_i * 512 + 512],
                        ).then_inc(s_in.h, 16)
                    em.on("sync", ldsl)
                    sl_at = s_in.inc(16)

                    def mmk(e, ssl=ssl, sl_at=sl_at, w_at=w_at, pe=prev_evac,
                            pv=hp_prev_v, first=(sl_i == 0)):
                        e.wait_ge(s_in.h, sl_at)
                        if first:
                            e.wait_ge(s_in.h, w_at)
                            e.wait_ge(s_in.h, tabs_at)
                            if pv:
                                e.wait_ge(s_v.h, pv)
                        if pe:
                            e.wait_ge(s_v.h, pe)
                        for c in range(8):
                            inst = e.matmul(bank[0][:],
                                            wk_sb[:, c * 128:c * 128 + 128],
                                            stream[ssl][:, c * 512:(c + 1) * 512],
                                            start=(c == 0), stop=(c == 7))
                            if c == 7:
                                inst.then_inc(s_t.h, 1)
                    em.on("tensor", mmk)
                    k_at = s_t.inc()

                    def evk(e, sl_i=sl_i, hp=hp, k_at=k_at):
                        e.wait_ge(s_t.h, k_at)
                        e.tensor_scalar(krope[:, sl_i * 512:(sl_i + 1) * 512], bank[0][:],
                                        bqk_sb[:, hp * 2 + 1: hp * 2 + 2], None, op0=ALU.add).then_inc(s_v.h, 1)
                    em.on("vector", evk)
                    prev_evac = s_v.inc()

                    for t in range(4):
                        vb = bank[1 + t % 2]

                        def mmv(e, ssl=ssl, t=t, vb=vb, vw=vband_free[t % 2]):
                            if vw:
                                e.wait_ge(s_v.h, vw)
                            for c in range(8):
                                inst = e.matmul(vb[:, 0:128],
                                                stream[ssl][:, c * 512 + t * 128: c * 512 + t * 128 + 128],
                                                wv_sb[:, c * 128:c * 128 + 128],
                                                start=(c == 0), stop=(c == 7))
                                if c == 7:
                                    inst.then_inc(s_t.h, 1)
                        em.on("tensor", mmv)
                        v_at = s_t.inc()
                        if t == 3:
                            stream_free[ssl] = v_at

                        def evv(e, sl_i=sl_i, hp=hp, t=t, vb=vb, v_at=v_at):
                            e.wait_ge(s_t.h, v_at)
                            gt = sl_i * 4 + t
                            for h in range(2):
                                inst = e.tensor_tensor(
                                    vhp[:, gt * 130 + h * 65: gt * 130 + h * 65 + 64],
                                    vb[0:128, h * 64:h * 64 + 64],
                                    bv_sb[:, hp * 128 + h * 64: hp * 128 + h * 64 + 64],
                                    op=ALU.add)
                            inst.then_inc(s_v.h, 1)
                        em.on("vector", evv)
                        vband_free[t % 2] = s_v.inc()
                        prev_evac = vband_free[t % 2]
                    if sl_i == 1:
                        emit_pend_div()
                    emit_pend_nmm()
                    emit_ropechain(sl_i, krope, kflip, cosk_sb, sink_sb, sl_i * 512, 512)
                    em.flush()

                for sl_i in range(4):
                    ssl = sl_i % 2

                    def ldslq(e, ssl=ssl, sl_i=sl_i, lw=stream_free[ssl]):
                        if lw:
                            e.wait_ge(s_t.h, lw)
                        e.dma_start(
                            out=stream[ssl].ap().rearrange("p (c t) -> p c t", c=8),
                            in_=xnT_own_h.rearrange("(c p) t -> p c t", p=128)[:, :, sl_i * 512:sl_i * 512 + 512],
                        ).then_inc(s_in.h, 16)
                    em.on("sync", ldslq)
                    sl_at = s_in.inc(16)

                    def mmq(e, ssl=ssl, sl_at=sl_at, pe=prev_evac):
                        e.wait_ge(s_in.h, sl_at)
                        e.wait_ge(s_v.h, pe)
                        for c in range(8):
                            inst = e.matmul(bank[0][:],
                                            wq_sb[:, c * 128:c * 128 + 128],
                                            stream[ssl][:, c * 512:(c + 1) * 512],
                                            start=(c == 0), stop=(c == 7))
                            if c == 7:
                                inst.then_inc(s_t.h, 1)
                    em.on("tensor", mmq)
                    q_at = s_t.inc()
                    stream_free[ssl] = q_at
                    emit_pend_nmm()

                    def evq(e, sl_i=sl_i, hp=hp, q_at=q_at):
                        e.wait_ge(s_t.h, q_at)
                        e.tensor_scalar(qrope[:, sl_i * 512:(sl_i + 1) * 512], bank[0][:],
                                        bqk_sb[:, hp * 2: hp * 2 + 1], None, op0=ALU.add).then_inc(s_v.h, 1)
                    em.on("vector", evq)
                    prev_evac = s_v.inc()
                    emit_ropechain(8 + sl_i, qrope, qflip, cosq_sb, sinq_sb, sl_i * 512, 512)
                    em.flush()
                drain_pend_nmm()
                proj_last_t = s_t.n

                def mcomb(e, rp=nred_prev):
                    e.wait_ge(s_v.h, rp)
                    e.drain()
                    e.tensor_reduce(mtmp[:, 0:1], mtmp[:, 2:10], axis=AX.X, op=ALU.max)
                    e.tensor_reduce(mtmp[:, 1:2], mtmp[:, 10:14], axis=AX.X, op=ALU.max)
                    e.drain()
                    e.tensor_scalar(mtmp[:, 14:15], mtmp[:, 0:1], 1.0, mtmp[:, 1:2],
                                    op0=ALU.mult, op1=ALU.add)
                    e.drain()
                    e.tensor_scalar(mtmp[:, 15:16], mtmp[:, 14:15], -1.05 / 16.0, None,
                                    op0=ALU.mult).then_inc(s_v.h, 1)
                em.on("vector", mcomb)
                mrow_at = s_v.inc()

                def mbc(e, mrow_at=mrow_at):
                    e.wait_ge(s_v.h, mrow_at)
                    e.matmul(bank[3][:, 300:301], ones1_sb[:], mtmp[:, 15:16],
                             start=True, stop=True).then_inc(s_t.h, 1)
                em.on("tensor", mbc)
                mbc_at = s_t.inc()

                def mev(e, mbc_at=mbc_at):
                    e.wait_ge(s_t.h, mbc_at)
                    e.tensor_copy(negM[:, 0:1], bank[3][:, 300:301]).then_inc(s_v.h, 1)
                em.on("vector", mev)
                negM_at = s_v.inc()
                em.flush()

                pend_av = None

                for m in range(4):
                    nkp = 8 * m + 8
                    for t in range(nkp):
                        par = t % 2
                        sA = bank[par]
                        sB = bank[2 + par]
                        pb = (psb[0][par], psb[1][par])
                        qsl = slice(m * 512, m * 512 + 512)
                        tau = t - 8 * m

                        smin = max(0, -((1 - tau) // 2)) if tau >= 0 else 0

                        def sc(e, t=t, m=m, smin=smin, sA=sA, sB=sB,
                               bpw=bankpair_free[par], negM_at=negM_at,
                               vones_at=vones_at, first=(m == 0 and t == 0)):
                            if first:
                                e.wait_ge(s_v.h, negM_at)
                                e.wait_ge(s_v.h, vones_at)
                            if bpw:
                                e.wait_ge(s_s.h, bpw)
                            ksl = slice(t * 128, t * 128 + 128)
                            for h, sX in ((0, sA), (1, sB)):
                                p0, p1 = 64 * h, 64 * h + 64
                                for s in range(smin, 4):
                                    qsl = slice((4 * m + s) * 128, (4 * m + s) * 128 + 128)
                                    inst = e.matmul(sX[:, s * 128:s * 128 + 128],
                                                    krope[p0:p1, ksl], qrope[p0:p1, qsl],
                                                    start=True, stop=True)
                                if h == 1:
                                    inst.then_inc(s_t.h, 1)
                        em.on("tensor", sc)
                        sc_at = s_t.inc()

                        def ex(e, sc_at=sc_at, sA=sA, sB=sB, pb=pb, smin=smin,
                               pav=av_buf_last[par]):
                            e.wait_ge(s_t.h, sc_at)
                            if pav:
                                e.wait_ge(s_t.h, pav)
                            e.activation(out=pb[0][:, smin * 128:512], in_=sA[:, smin * 128:512],
                                         func=AF.Exp, scale=ATTN_SCALE, bias=negM[:, 0:1])
                            e.activation(out=pb[1][:, smin * 128:512], in_=sB[:, smin * 128:512],
                                         func=AF.Exp, scale=ATTN_SCALE, bias=negM[:, 0:1]).then_inc(s_s.h, 1)
                        em.on("scalar", ex)
                        exp_at = s_s.inc()
                        bankpair_free[par] = exp_at

                        vops = []
                        if tau >= 0:
                            if tau % 2 == 0 and tau >= 2:
                                vops.append(("zero", (tau - 2) // 2))
                            if tau % 2 == 1 and tau >= 3:
                                vops.append(("zero", (tau - 3) // 2))
                            if 0 <= tau <= 7:
                                vops.append(("mask", tau))
                        mask_at = None
                        if vops:
                            def mk(e, exat=exp_at, vops=tuple(vops), pb=pb):
                                e.wait_ge(s_s.h, exat)
                                last = None
                                for kind, a in vops:
                                    if kind == "zero":
                                        for h in range(2):
                                            last = e.memset(pb[h][:, a * 128:a * 128 + 128], 0.0)
                                    else:
                                        mm_ = m1_sb if a % 2 == 0 else m2_sb
                                        blk = a // 2
                                        for h in range(2):
                                            last = e.tensor_tensor(pb[h][:, blk * 128:blk * 128 + 128],
                                                                   pb[h][:, blk * 128:blk * 128 + 128],
                                                                   mm_[:], op=ALU.mult)
                                last.then_inc(s_v.h, 1)
                            em.on("vector", mk)
                            mask_at = s_v.inc()

                        if pend_div is not None and t >= 1:
                            pend_div()
                            pend_div = None
                        if pend_av is not None:
                            pend_av()
                            pend_av = None

                        def make_av(t=t, exat=exp_at, mask_at=mask_at, pb=pb,
                                    nkp=nkp, m=m):
                            def emit():
                                nonlocal av_done_last
                                zw = zfree_par[m % 2]

                                def av(e, t=t, exat=exat, mask_at=mask_at, pb=pb,
                                       nkp=nkp, zw=zw, zb0=4 + 2 * (m % 2)):
                                    e.wait_ge(s_s.h, exat)
                                    if mask_at:
                                        e.wait_ge(s_v.h, mask_at)
                                    if t == 0 and zw:
                                        e.wait_ge(s_v.h, zw)
                                    for h in range(2):
                                        inst = e.matmul(bank[zb0 + h][0:65, 0:512],
                                                        vhp[:, t * 130 + h * 65: t * 130 + h * 65 + 65],
                                                        pb[h][:],
                                                        start=(t == 0), stop=(t == nkp - 1))
                                        if h == 1:
                                            inst.then_inc(s_t.h, 1)
                                em.on("tensor", av)
                                avat = s_t.inc()
                                av_buf_last[t % 2] = avat
                                av_done_last = avat
                            return emit
                        pend_av = make_av()

                    def make_div(m=m, hp=hp, zb0=4 + 2 * (m % 2)):
                        def emit():
                            nonlocal znT_free, zfree_par
                            av_done = av_done_last

                            def zev(e, av_done=av_done, zf=znT_free, zb0=zb0):
                                e.wait_ge(s_t.h, av_done)
                                if zf:
                                    e.wait_ge(s_out.h, zf)
                                e.tensor_copy(znT_sl[0][:], bank[zb0][0:65, 0:512])
                                e.tensor_copy(znT_sl[1][:], bank[zb0 + 1][0:65, 0:512]).then_inc(s_v.h, 1)
                            em.on("vector", zev)
                            zev_at = s_v.inc()
                            zfree_par[m % 2] = zev_at

                            def zst(e, za=zev_at, hp=hp, m=m):
                                e.wait_ge(s_v.h, za)
                                for h in range(2):
                                    e.dma_start(out=znT_h[hp * 128 + h * 64: hp * 128 + h * 64 + 64,
                                                          m * 512:m * 512 + 512],
                                                in_=znT_sl[h][0:64, :]).then_inc(s_out.h, 16)
                                    e.dma_start(out=l_h[h:h + 1, hp * 2048 + m * 512: hp * 2048 + m * 512 + 512],
                                                in_=znT_sl[h][64:65, :]).then_inc(s_out.h, 16)
                            em.on("gpsimd", zst)
                            znT_free = s_out.inc(64)
                        return emit
                    pend_div = make_div()
                    em.flush()

                if pend_av is not None:
                    pend_av()
                    pend_av = None
                em.flush()

                hp_prev_v = s_v.n
            if pend_div is not None:
                pend_div()
                pend_div = None
            em.flush()
            zn_stores = s_out.n
            p2_final_v = s_v.n

        p35 = ctx.enter_context(ExitStack())
        acc_sb = p35.enter_context(nc.sbuf_tensor("acc_sb", [128, NT_OWN * DM], F32))

        with ExitStack() as p3:
            sb3 = lambda n, shp: p3.enter_context(nc.sbuf_tensor(n, shp, F32))
            bo_sb = sb3("p3_bo", [128, DM])
            znT_sb = sb3("p3_znT", [128, 8 * SOWN])
            wo_sb = sb3("p3_wo", [128, 8 * DM])
            xs = [sb3(f"p3_x{i}", [128, DM]) for i in range(2)]
            hsel_sb = sb3("p3_hsel", [2, 128])
            l_sb = [sb3(f"p3_l{i}", [2, SOWN]) for i in range(2)]
            linv_sb = [sb3("p3_linv", [2, SOWN])] * 2
            lbc_sb = sb3("p3_lbc", [128, 512])

            def p3ld(e, zn_stores=zn_stores, pv=p2_final_v):
                e.wait_ge(s_out.h, zn_stores)
                e.wait_ge(s_v.h, pv)
                e.dma_start(out=znT_sb.ap().rearrange("p (c t) -> p c t", c=8),
                            in_=znT_h.rearrange("(c p) t -> p c t", p=128)).then_inc(s_in.h, 16)
                e.dma_start(out=wo_sb.ap().rearrange("p (c m) -> p c m", c=8),
                            in_=wo.rearrange("(c p) m -> p c m", p=128)).then_inc(s_in.h, 16)
                e.dma_start(out=bo_sb[:], in_=bo_rep[:]).then_inc(s_in.h, 16)
                e.dma_start(out=hsel_sb[:], in_=halfsel[:]).then_inc(s_in.h, 16)
            em.on("sync", p3ld)
            p3_at = s_in.inc(64)

            div_chunk = {}
            last_lbcmm = 0
            for c in range(8):
                sl3 = c % 2

                def lload(e, c=c, sl3=sl3, pv=div_chunk.get(c - 1, 0)):
                    if c == 0:
                        e.wait_ge(s_out.h, zn_stores)
                    if pv:
                        e.wait_ge(s_v.h, pv)
                    e.dma_start(out=l_sb[sl3][:], in_=l_h[:, c * 2048:(c + 1) * 2048]).then_inc(s_in.h, 16)
                em.on("sync", lload)
                l_at = s_in.inc(16)

                def p3rec(e, l_at=l_at, sl3=sl3, mw=last_lbcmm):
                    e.wait_ge(s_in.h, l_at)
                    if mw:
                        e.wait_ge(s_t.h, mw)
                    e.reciprocal(linv_sb[sl3][:], l_sb[sl3][:]).then_inc(s_v.h, 1)
                em.on("vector", p3rec)
                rec_at = s_v.inc()

                nb_done = {}
                for nb in range(4):
                    bk = bank[4 + 2 * sl3 + nb % 2]

                    def lbcmm(e, c=c, sl3=sl3, nb=nb, bk=bk, rec_at=rec_at,
                              p3_at=p3_at, pv=div_chunk.get(c - 1, 0),
                              bw=nb_done.get(nb - 2, 0)):
                        if c == 0 and nb == 0:
                            e.wait_ge(s_in.h, p3_at)
                        if nb == 0:
                            e.wait_ge(s_v.h, rec_at)
                            if pv:
                                e.wait_ge(s_v.h, pv)
                        if bw:
                            e.wait_ge(s_v.h, bw)
                        e.matmul(bk[:], hsel_sb[:], linv_sb[sl3][:, nb * 512: nb * 512 + 512],
                                 start=True, stop=True).then_inc(s_t.h, 1)
                    em.on("tensor", lbcmm)
                    bc_at = s_t.inc()

                    def lbcev(e, c=c, sl3=sl3, nb=nb, bk=bk, bc_at=bc_at, p3_at=p3_at):
                        if c == 0 and nb == 0:
                            e.wait_ge(s_in.h, p3_at)
                        e.wait_ge(s_t.h, bc_at)
                        e.tensor_copy(lbc_sb[:], bk[:])
                        e.drain()
                        e.tensor_tensor(znT_sb[:, c * 2048 + nb * 512: c * 2048 + nb * 512 + 512],
                                        znT_sb[:, c * 2048 + nb * 512: c * 2048 + nb * 512 + 512],
                                        lbc_sb[:], op=ALU.mult).then_inc(s_v.h, 1)
                    em.on("vector", lbcev)
                    nb_done[nb] = s_v.inc()
                    if nb == 3:
                        div_chunk[c] = nb_done[nb]
                        last_lbcmm = bc_at

            div_done = div_chunk[7]
            em.flush()

            x_free = [0, 0]
            o_evo_hist = {}
            for i in range(NT_OWN):
                slot = i % 2

                def ldx(e, i=i, slot=slot, lw=x_free[slot], p3_at=p3_at):
                    if lw:
                        e.wait_ge(s_v.h, lw)
                    e.dma_start(out=xs[slot][:], in_=x_own[i * 128:(i + 1) * 128, :]).then_inc(s_in.h, 16)
                em.on("sync", ldx)
                x_at = s_in.inc(16)

                def mmo(e, i=i, p3_at=p3_at, dv=div_done, bw=o_evo_hist.get(i - 2)):
                    if i == 0:
                        e.wait_ge(s_in.h, p3_at)
                        e.wait_ge(s_v.h, dv)
                    if bw:
                        e.wait_ge(s_v.h, bw)
                    for n in range(2):
                        for c in range(8):
                            inst = e.matmul(bank[2 * (i % 2) + n][:],
                                            znT_sb[:, c * 2048 + i * 128: c * 2048 + i * 128 + 128],
                                            wo_sb[:, c * 1024 + n * 512: c * 1024 + n * 512 + 512],
                                            start=(c == 0), stop=(c == 7))
                            if c == 7:
                                inst.then_inc(s_t.h, 1)
                em.on("tensor", mmo)
                o_at = [s_t.inc(), s_t.inc()]

                def evo(e, i=i, slot=slot, x_at=x_at, o_at=tuple(o_at)):
                    e.wait_ge(s_in.h, x_at)
                    for n in range(2):
                        e.wait_ge(s_t.h, o_at[n])
                        dsl = slice(i * DM + n * 512, i * DM + n * 512 + 512)
                        e.tensor_tensor(acc_sb[:, dsl], bank[2 * (i % 2) + n][:],
                                        bo_sb[:, n * 512:n * 512 + 512], op=ALU.add)
                        inst = e.tensor_tensor(acc_sb[:, dsl], acc_sb[:, dsl],
                                               xs[slot][:, n * 512:n * 512 + 512], op=ALU.add)
                    inst.then_inc(s_v.h, 1)
                em.on("vector", evo)
                x_free[slot] = s_v.inc()
                o_evo_hist[i] = x_free[slot]
                if i % 4 == 3:
                    em.flush()
            em.flush()
            acc_seeded = s_v.n
            if DEBUG_TAPS:
                def acc3st(e, ac=acc_seeded):
                    e.wait_ge(s_v.h, ac)
                    for t in range(NT_OWN):
                        e.dma_start(out=acc3_h[t * 128:(t + 1) * 128, :],
                                    in_=acc_sb[:, t * DM:(t + 1) * DM]).then_inc(s_out.h, 16)
                em.on("gpsimd", acc3st)
                s_out.inc(16 * NT_OWN)
                em.flush()

        with ExitStack() as p5:
            sb5 = lambda n, shp: p5.enter_context(nc.sbuf_tensor(n, shp, F32))
            xnTo = sb5("p5_xnTo", [128, 8 * SOWN])
            ks = [sb5(f"p5_k{i}", [128, 8 * 512]) for i in range(2)]
            vs = [sb5(f"p5_v{i}", [128, 4 * DM]) for i in range(2)]
            hg = sb5("p5_hg", [128, 512])
            hgT = sb5("p5_hgT", [128, 512])

            def p5ld0(e, ac=acc_seeded):
                e.wait_ge(s_v.h, ac)
                e.dma_start(out=xnTo.ap().rearrange("p (c t) -> p c t", c=8),
                            in_=xnT_own_h.rearrange("(c p) t -> p c t", p=128)).then_inc(s_in.h, 16)
            em.on("sync", p5ld0)
            xnTo_at = s_in.inc(16)

            ks_free = [0, 0]
            store_after = [0] * NT_OWN
            gates_dep = gates_at
            hg_at_hist = {}
            tp_at_hist = {}

            for u in range(NUB):
                su = u % 2

                def ldu(e, u=u, su=su, kw=ks_free[su]):
                    if kw:
                        e.wait_ge(s_t.h, kw)
                    e.dma_start(out=ks[su].ap().rearrange("p (c u) -> p c u", c=8),
                                in_=keys_l.rearrange("(c p) u -> p c u", p=128)[:, :, u * 512:u * 512 + 512],
                                ).then_inc(s_in.h, 16)
                    e.dma_start(out=vs[su].ap().rearrange("p (s m) -> p s m", s=4),
                                in_=values_l.rearrange("(s p) m -> p s m", p=128)[:, u * 4:u * 4 + 4, :],
                                ).then_inc(s_in.h, 16)
                em.on("sync", ldu)
                k_at = s_in.inc(16)
                v_at = s_in.inc(16)

                pend5 = None
                for t in range(NT_OWN + 1):
                    if t < NT_OWN:
                        hb = bank[t % 2]

                        def mmh(e, u=u, su=su, t=t, hb=hb, k_at=k_at, xnTo_at=xnTo_at,
                                bw=hg_at_hist.get(t - 2), acc_seeded=acc_seeded):
                            if u == 0 and t == 0:
                                e.wait_ge(s_in.h, xnTo_at)
                                e.wait_ge(s_v.h, acc_seeded)
                            if t == 0:
                                e.wait_ge(s_in.h, k_at)
                            if bw:
                                e.wait_ge(s_v.h, bw)
                            for c in range(8):
                                inst = e.matmul(hb[:], xnTo[:, c * 2048 + t * 128: c * 2048 + t * 128 + 128],
                                                ks[su][:, c * 512:(c + 1) * 512],
                                                start=(c == 0), stop=(c == 7))
                                if c == 7:
                                    inst.then_inc(s_t.h, 1)
                        em.on("tensor", mmh)
                        h_at = s_t.inc()

                    if pend5 is not None:
                        pend5()
                        pend5 = None

                    if t < NT_OWN:
                        def evh(e, u=u, t=t, hb=hb, h_at=h_at, gdep=gates_dep,
                                tpw=tp_at_hist.get(t - 1)):
                            if u == 0 and t == 0 and gdep:
                                e.wait_ge(s_v.h, gdep)
                            e.wait_ge(s_t.h, h_at)
                            if tpw:
                                e.wait_ge(s_t.h, tpw)
                            for i in range(4):
                                eidx = u * 4 + i
                                inst = e.tensor_scalar(hg[:, i * 128:(i + 1) * 128], hb[:, i * 128:(i + 1) * 128],
                                                       gates_sb[:, t * 64 + eidx: t * 64 + eidx + 1], 0.0,
                                                       op0=ALU.mult, op1=ALU.max)
                            inst.then_inc(s_v.h, 1)
                        em.on("vector", evh)
                        hg_at_hist[t] = s_v.inc()

                        def make_deferred(u=u, su=su, t=t, v_at=v_at, hg_at=hg_at_hist[t]):
                            def emit():
                                tb = bank[2 + t % 2]

                                def tph(e, tb=tb, hg_at=hg_at):
                                    e.wait_ge(s_v.h, hg_at)
                                    for i in range(4):
                                        inst = e.transpose(tb[:, i * 128:(i + 1) * 128],
                                                           hg[:, i * 128:(i + 1) * 128], ident_sb[:])
                                        if i == 3:
                                            inst.then_inc(s_t.h, 1)
                                em.on("tensor", tph)
                                tp_at = s_t.inc()
                                tp_at_hist[t] = tp_at

                                def evT(e, tb=tb, tp_at=tp_at):
                                    e.wait_ge(s_t.h, tp_at)
                                    e.tensor_copy(hgT[:], tb[:]).then_inc(s_v.h, 1)
                                em.on("vector", evT)
                                T_at = s_v.inc()

                                def mm2(e, su=su, t=t, v_at=v_at, T_at=T_at):
                                    if t == 0:
                                        e.wait_ge(s_in.h, v_at)
                                    e.wait_ge(s_v.h, T_at)
                                    for n in range(2):
                                        for s4 in range(4):
                                            inst = e.matmul(bank[4 + 2 * (t % 2) + n][:],
                                                            hgT[:, s4 * 128:(s4 + 1) * 128],
                                                            vs[su][:, s4 * 1024 + n * 512: s4 * 1024 + n * 512 + 512],
                                                            start=(s4 == 0), stop=(s4 == 3))
                                            if s4 == 3:
                                                inst.then_inc(s_t.h, 1)
                                em.on("tensor", mm2)
                                f_at = [s_t.inc(), s_t.inc()]

                                def addacc(e, t=t, f_at=tuple(f_at)):
                                    for n in range(2):
                                        e.wait_ge(s_t.h, f_at[n])
                                        dsl = slice(t * DM + n * 512, t * DM + n * 512 + 512)
                                        inst = e.scalar_tensor_tensor(acc_sb[:, dsl],
                                                                      bank[4 + 2 * (t % 2) + n][:], 1.0,
                                                                      acc_sb[:, dsl], op0=ALU.mult, op1=ALU.add)
                                    inst.then_inc(s_v.h, 1)
                                em.on("vector", addacc)
                                add_at = s_v.inc()
                                if u == NUB - 1:
                                    store_after[t] = add_at
                            return emit
                        pend5 = make_deferred()
                    if t % 4 == 3:
                        em.flush()
                ks_free[su] = s_t.n
                em.flush()

            def stout(e, store_after=tuple(store_after), base=s_out.n):
                for t in range(NT_OWN):
                    e.wait_ge(s_v.h, store_after[t])
                    e.dma_start(out=out[t * 128:(t + 1) * 128, :],
                                in_=acc_sb[:, t * DM:(t + 1) * DM]).then_inc(s_out.h, 16)
                e.wait_ge(s_out.h, base + 16 * NT_OWN)
            em.on("gpsimd", stout)
            s_out.inc(16 * NT_OWN)
            em.flush()

    return nc


def make_inmaps(inputs):
    x = np.asarray(inputs["L0_attn_x"], dtype=np.float32)
    WQ = np.asarray(inputs["L0_attn_W_Q"], dtype=np.float32)
    WK = np.asarray(inputs["L0_attn_W_K"], dtype=np.float32)
    WV = np.asarray(inputs["L0_attn_W_V"], dtype=np.float32)
    WO = np.asarray(inputs["L0_attn_W_O"], dtype=np.float32)
    bQ = np.asarray(inputs["L0_attn_b_Q"], dtype=np.float32)
    bK = np.asarray(inputs["L0_attn_b_K"], dtype=np.float32)
    bV = np.asarray(inputs["L0_attn_b_V"], dtype=np.float32)
    bO = np.asarray(inputs["L0_attn_b_O"], dtype=np.float32)
    esel = np.asarray(inputs["L0_ffn_expert_sel"], dtype=np.float32)
    keys = np.asarray(inputs["L0_ffn_keys"], dtype=np.float32)
    values = np.asarray(inputs["L0_ffn_values"], dtype=np.float32)

    wq_l = np.ascontiguousarray(WQ.transpose(1, 0, 2).reshape(DM, DM))
    wk_l = np.ascontiguousarray(WK.transpose(1, 0, 2).reshape(DM, DM))
    wv_l = np.ascontiguousarray(WV.transpose(1, 0, 2).reshape(DM, DM))
    wo_l = np.ascontiguousarray(WO.reshape(DM, DM))
    selT = np.ascontiguousarray(esel.T)
    keys_l = np.ascontiguousarray(keys.transpose(1, 0, 2).reshape(DM, UDIM))
    values_l = np.ascontiguousarray(values.reshape(UDIM, DM))

    bqk = np.zeros((128, 16), np.float32)
    for hp in range(HP):
        for h in range(2):
            bqk[h * 64:(h + 1) * 64, hp * 2 + 0] = bQ[2 * hp + h]
            bqk[h * 64:(h + 1) * 64, hp * 2 + 1] = bK[2 * hp + h]
    bv_rep = np.ascontiguousarray(np.broadcast_to(bV.reshape(1, DM), (128, DM)))
    bo_rep = np.ascontiguousarray(np.broadcast_to(bO.reshape(1, DM), (128, DM)))

    pos = np.arange(S, dtype=np.float64)
    jj = np.arange(32, dtype=np.float64)
    freq = 10000.0 ** (jj / 32.0)
    ang = pos[:, None] / freq[None, :]
    cos32 = np.cos(ang)
    sin32 = np.sin(ang)
    cosh = np.concatenate([cos32, cos32], axis=1).T
    sinh = np.concatenate([-sin32, sin32], axis=1).T
    cos_k = np.ascontiguousarray(np.concatenate([cosh, cosh], 0), np.float32) \
        .astype(np.float32)
    sin_k = np.ascontiguousarray(np.concatenate([sinh, sinh], 0)).astype(np.float32)

    identm = np.eye(128, dtype=np.float32)
    hsel = np.zeros((2, 128), np.float32)
    hsel[0, 0:64] = 1.0
    hsel[1, 64:128] = 1.0
    tri = np.triu(np.ones((128, 128), np.float32))
    onesm = np.ones((128, 128), np.float32)
    zerosm = np.zeros((128, 128), np.float32)

    in_maps = []
    own_rows_all = []
    for core in range(8):
        b, r = core // 2, core % 2
        gs = [2 * j + (1 - r) for j in range(16)]
        own_rows = np.concatenate([np.arange(128 * g, 128 * g + 128) for g in gs])
        own_rows_all.append(own_rows)
        x_b = np.ascontiguousarray(x[b])
        x_ownc = np.ascontiguousarray(x_b[own_rows])
        cos_q = np.ascontiguousarray(cos_k[:, own_rows])
        sin_q = np.ascontiguousarray(sin_k[:, own_rows])
        bmask = np.stack([onesm, tri] if r == 0 else [tri, zerosm])
        in_maps.append({
            "x_full": x_b, "x_own": x_ownc,
            "wq": wq_l, "wk": wk_l, "wv": wv_l, "wo": wo_l,
            "bqk": bqk, "bv_rep": bv_rep, "bo_rep": bo_rep,
            "selT": selT, "keys_l": keys_l, "values_l": values_l,
            "cos_k": cos_k, "sin_k": sin_k, "cos_q": cos_q, "sin_q": sin_q,
            "ident": identm, "halfsel": hsel, "bmask": np.ascontiguousarray(bmask),
        })
    return in_maps, own_rows_all


def kernel(**inputs):
    in_maps, own_rows_all = make_inmaps(inputs)
    nc = build_program()
    res = run_bass_kernel_spmd(nc, in_maps, core_ids=list(range(8)))
    out = np.empty((B, S, DM), dtype=np.float32)
    for core in range(8):
        b = core // 2
        out[b].reshape(S, DM)[own_rows_all[core]] = res.results[core]["out"]
    return out


# version 19 (final)
# speedup vs baseline: 1.0143x; 1.0143x over previous
import numpy as np
from contextlib import ExitStack

import concourse.bass as bass
import concourse.mybir as mybir
from concourse.bass_utils import run_bass_kernel_spmd

F32 = mybir.dt.float32
F32R = mybir.dt.float32r
AF = mybir.ActivationFunctionType
ALU = mybir.AluOpType
AX = mybir.AxisListType

B, S, DM = 4, 4096, 1024
NH, DH, HP = 16, 64, 8
SOWN = 2048
NT_FULL, NT_OWN = S // 128, SOWN // 128
NEXP, ESZ, UDIM = 64, 128, 8192
NUB = 16
EPS = 1e-5
ATTN_SCALE = 0.125

DEBUG_TAPS = False


class Sem:
    def __init__(self, h):
        self.h = h
        self.n = 0

    def inc(self, k=1):
        self.n += k
        return self.n


class Emitter:
    ENGINES = ("sync", "tensor", "vector", "scalar", "gpsimd")

    def __init__(self, block):
        self.block = block
        self.pend = {k: [] for k in self.ENGINES}

    def on(self, eng, fn):
        self.pend[eng].append(fn)

    def flush(self):
        for eng in self.ENGINES:
            fns = self.pend[eng]
            if not fns:
                continue
            self.pend[eng] = []
            getattr(self.block, eng)(lambda e, fns=fns: [f(e) for f in fns])


def build_program():
    nc = bass.Bass("TRN2", target_bir_lowering=False)
    dt = lambda name, shape, kind: nc.dram_tensor(name, shape, F32, kind=kind).ap()

    x_full = dt("x_full", [S, DM], "ExternalInput")
    x_own = dt("x_own", [SOWN, DM], "ExternalInput")
    wq = dt("wq", [DM, DM], "ExternalInput")
    wk = dt("wk", [DM, DM], "ExternalInput")
    wv = dt("wv", [DM, DM], "ExternalInput")
    wo = dt("wo", [DM, DM], "ExternalInput")
    bqk = dt("bqk", [128, 16], "ExternalInput")
    bv_rep = dt("bv_rep", [128, DM], "ExternalInput")
    bo_rep = dt("bo_rep", [128, DM], "ExternalInput")
    selT_d = dt("selT", [DM, NEXP], "ExternalInput")
    keys_l = dt("keys_l", [DM, UDIM], "ExternalInput")
    values_l = dt("values_l", [UDIM, DM], "ExternalInput")
    cos_k = dt("cos_k", [128, S], "ExternalInput")
    sin_k = dt("sin_k", [128, S], "ExternalInput")
    cos_q = dt("cos_q", [128, SOWN], "ExternalInput")
    sin_q = dt("sin_q", [128, SOWN], "ExternalInput")
    ident = dt("ident", [128, 128], "ExternalInput")
    bmask = dt("bmask", [2, 128, 128], "ExternalInput")

    out = dt("out", [SOWN, DM], "ExternalOutput")
    skind = "ExternalOutput" if DEBUG_TAPS else "Internal"
    xnT_full_h = dt("xnT_full_h", [DM, S], skind)
    xnT_own_h = dt("xnT_own_h", [DM, SOWN], skind)
    znT_h = dt("znT_h", [DM, SOWN], skind)
    gates_h = dt("gates_h", [128, NT_OWN * 64], skind) if DEBUG_TAPS else None

    ctx = ExitStack()
    with ctx:
        sb = lambda name, shape: ctx.enter_context(nc.sbuf_tensor(name, shape, F32))
        ident_sb = sb("ident_sb", [128, 128])
        ones1_sb = sb("ones1_sb", [1, 128])
        gates_sb = sb("gates_sb", [128, NT_OWN * 64])

        bank = [ctx.enter_context(nc.psum_tensor(f"bank{i}", [128, 512], F32))
                for i in range(8)]

        s_in = Sem(ctx.enter_context(nc.semaphore("s_in")))
        s_out = Sem(ctx.enter_context(nc.semaphore("s_out")))
        s_t = Sem(ctx.enter_context(nc.semaphore("s_t")))
        s_v = Sem(ctx.enter_context(nc.semaphore("s_v")))
        s_s = Sem(ctx.enter_context(nc.semaphore("s_s")))

        block = ctx.enter_context(nc.Block())
        em = Emitter(block)

        def p0(e):
            e.dma_start(out=ident_sb[:], in_=ident[:]).then_inc(s_in.h, 16)
        em.on("sync", p0)
        consts_at = s_in.inc(16)

        def p0v(e):
            e.memset(ones1_sb[:], 1.0).then_inc(s_v.h, 1)
        em.on("vector", p0v)
        s_v.inc()

        with ExitStack() as p1:
            sb1 = lambda n, shp: p1.enter_context(nc.sbuf_tensor(n, shp, F32))
            NP1 = 4
            xt = [sb1(f"p1_x{i}", [128, DM]) for i in range(NP1)]
            xsq = [sb1("p1_xsq", [128, DM])]
            p1dum = sb1("p1_dum", [128, 1])
            stat = [sb1(f"p1_stat{i}", [128, 8]) for i in range(2)]
            xnt = [sb1(f"p1_xn{i}", [128, DM]) for i in range(NP1)]
            xnTt = [sb1(f"p1_xnT{i}", [128, DM]) for i in range(NP1)]
            logit_sb = sb1("p1_logit", [128, NT_OWN * 64])
            selT_sb = sb1("p1_selT", [128, 8 * 64])
            max8 = sb1("p1_max8", [128, 8])
            gmask = sb1("p1_gmask", [128, 64])

            def ld_selT(e):
                e.dma_start(out=selT_sb.ap().rearrange("p (c e) -> p c e", c=8),
                            in_=selT_d.rearrange("(c p) e -> p c e", p=128)).then_inc(s_in.h, 16)
            em.on("sync", ld_selT)
            selT_at = s_in.inc(16)

            def p1t0(e):
                e.wait_ge(s_in.h, consts_at)
                e.wait_ge(s_in.h, selT_at)
            em.on("tensor", p1t0)

            xt_free = [0] * NP1
            xn_free = [0] * NP1
            xnT_free = [0] * NP1
            tbank_free = 0
            tbank_free_prev = 0
            lbank_free = 0
            stat_sfree = [0, 0]

            def p1_phase1(i, src_ap):
                nonlocal stat_sfree
                slot = i % NP1

                def ld(e, slot=slot, src_ap=src_ap, lw=xt_free[slot]):
                    if lw:
                        e.wait_ge(s_v.h, lw)
                    e.dma_start(out=xt[slot][:], in_=src_ap).then_inc(s_in.h, 16)
                em.on("sync", ld)
                x_at = s_in.inc(16)

                st_ = stat[i % 2]

                def sq(e, slot=slot, x_at=x_at, st_=st_, lw=stat_sfree[i % 2]):
                    e.wait_ge(s_in.h, x_at)
                    if lw:
                        e.wait_ge(s_v.h, lw)
                    e.activation(out=xsq[0][:], in_=xt[slot][:], func=AF.Square,
                                 accum_out=st_[:, 1:2]).then_inc(s_s.h, 1)
                em.on("scalar", sq)
                sq_at = s_s.inc()

                def ln(e, slot=slot, x_at=x_at, sq_at=sq_at, st_=st_):
                    e.wait_ge(s_in.h, x_at)
                    e.reduce_sum(st_[:, 0:1], xt[slot][:], axis=AX.X)
                    e.drain()
                    e.tensor_scalar(st_[:, 2:3], st_[:, 0:1], 1.0 / DM, None, op0=ALU.mult)
                    e.drain()
                    e.tensor_scalar(st_[:, 3:4], st_[:, 2:3], st_[:, 2:3], None, op0=ALU.mult)
                    e.drain()
                    e.wait_ge(s_s.h, sq_at)
                    e.tensor_scalar(st_[:, 4:5], st_[:, 1:2], 1.0 / DM, st_[:, 3:4],
                                    op0=ALU.mult, op1=ALU.subtract)
                    e.drain()
                    e.tensor_scalar(st_[:, 5:6], st_[:, 4:5], EPS, None, op0=ALU.add)
                    e.drain()
                    e.reciprocal(st_[:, 6:7], st_[:, 5:6]).then_inc(s_v.h, 1)
                em.on("vector", ln)
                rv_at = s_v.inc()
                stat_sfree[i % 2] = rv_at

                def rsq(e, rv_at=rv_at, st_=st_):
                    e.wait_ge(s_v.h, rv_at)
                    e.activation(out=st_[:, 7:8], in_=st_[:, 6:7], func=AF.Sqrt).then_inc(s_s.h, 1)
                em.on("scalar", rsq)
                inv_at = s_s.inc()
                return (i, slot, inv_at)

            def p1_phase2(info, dst_h, own_idx):
                nonlocal tbank_free, lbank_free, tbank_free_prev
                i, slot, inv_at = info
                st_ = stat[i % 2]
                def norm(e, slot=slot, inv_at=inv_at, lw=xn_free[slot], st_=st_):
                    e.wait_ge(s_s.h, inv_at)
                    if lw:
                        e.wait_ge(s_t.h, lw)
                    e.tensor_scalar(xnt[slot][:], xt[slot][:], st_[:, 2:3], st_[:, 7:8],
                                    op0=ALU.subtract, op1=ALU.mult).then_inc(s_v.h, 1)
                em.on("vector", norm)
                xn_at = s_v.inc()
                xt_free[slot] = xn_at

                tb0 = 2 * (i % 2)

                def tps(e, slot=slot, xn_at=xn_at, bw=tbank_free, tb0=tb0):
                    e.wait_ge(s_v.h, xn_at)
                    if bw:
                        e.wait_ge(s_v.h, bw)
                    for c in range(8):
                        inst = e.transpose(bank[tb0 + c // 4][:, (c % 4) * 128:(c % 4) * 128 + 128],
                                           xnt[slot][:, c * 128:c * 128 + 128], ident_sb[:])
                        if c in (3, 7):
                            inst.then_inc(s_t.h, 1)
                em.on("tensor", tps)
                tp_half = [s_t.inc(), s_t.inc()]

                def ev_s(e, slot=slot, tp0=tp_half[0], lw=xnT_free[slot], tb0=tb0):
                    if lw:
                        e.wait_ge(s_out.h, lw)
                    e.wait_ge(s_t.h, tp0)
                    e.activation(out=xnTt[slot][:, 0:512], in_=bank[tb0][:],
                                 func=AF.Copy, scale=1.0).then_inc(s_s.h, 1)
                em.on("scalar", ev_s)
                evs_at = s_s.inc()

                def ev(e, slot=slot, tp1=tp_half[1], lw=xnT_free[slot], tb0=tb0,
                       evs_at=evs_at):
                    if lw:
                        e.wait_ge(s_out.h, lw)
                    e.wait_ge(s_t.h, tp1)
                    e.tensor_copy(xnTt[slot][:, 512:1024], bank[tb0 + 1][:])
                    e.wait_ge(s_s.h, evs_at)
                    e.tensor_copy(p1dum[:], p1dum[:]).then_inc(s_v.h, 1)
                em.on("vector", ev)
                xnT_at = s_v.inc()
                tbank_free, tbank_free_prev = tbank_free_prev, xnT_at

                lg_at = None
                if own_idx is not None:
                    def lgmm(e, slot=slot, xnT_at=xnT_at, lbw=lbank_free):
                        e.wait_ge(s_v.h, xnT_at)
                        if lbw:
                            e.wait_ge(s_s.h, lbw)
                        for c in range(8):
                            inst = e.matmul(bank[4][:, 0:64], xnTt[slot][:, c * 128:c * 128 + 128],
                                            selT_sb[:, c * 64:c * 64 + 64],
                                            start=(c == 0), stop=(c == 7))
                            if c == 7:
                                inst.then_inc(s_t.h, 1)
                    em.on("tensor", lgmm)
                    lg_at = s_t.inc()

                    def lgev(e, lg_at=lg_at, own_idx=own_idx):
                        e.wait_ge(s_t.h, lg_at)
                        e.activation(out=logit_sb[:, own_idx * 64:own_idx * 64 + 64],
                                     in_=bank[4][:, 0:64], func=AF.Copy, scale=1.0).then_inc(s_s.h, 1)
                    em.on("scalar", lgev)
                    lbank_free = s_s.inc()
                xn_free[slot] = lg_at if lg_at is not None else tp_half[1]

                def stv(e, slot=slot, xnT_at=xnT_at, dst_h=dst_h, i=i,
                        lg_at=lg_at):
                    e.wait_ge(s_v.h, xnT_at)
                    e.dma_start(out=dst_h.rearrange("(c p) t -> p c t", p=128)[:, :, i * 128:i * 128 + 128],
                                in_=xnTt[slot].ap().rearrange("p (c t) -> p c t", c=8)).then_inc(s_out.h, 16)
                em.on("gpsimd", stv)
                xnT_free[slot] = s_out.inc(16)

            p1_pend = None
            for i in range(NT_FULL + NT_OWN):
                if i < NT_FULL:
                    info = p1_phase1(i, x_full[i * 128:(i + 1) * 128, :])
                    nxt = (info, xnT_full_h, None)
                else:
                    j = i - NT_FULL
                    info = p1_phase1(NT_FULL + j, x_own[j * 128:(j + 1) * 128, :])
                    info = (j, info[1], info[2])
                    nxt = (info, xnT_own_h, j)
                if p1_pend is not None:
                    p1_phase2(*p1_pend)
                p1_pend = nxt
                if i % 4 == 3:
                    em.flush()
            p1_phase2(*p1_pend)
            p1_pend = None

            lg_done = s_s.n

            def sig(e, lg_done=lg_done):
                e.wait_ge(s_s.h, lg_done)
                e.activation(out=logit_sb[:], in_=logit_sb[:], func=AF.Sigmoid).then_inc(s_s.h, 1)
            em.on("scalar", sig)
            sig_at = s_s.inc()

            def gates(e, sig_at=sig_at):
                e.wait_ge(s_s.h, sig_at)
                for i in range(NT_OWN):
                    sl = slice(i * 64, i * 64 + 64)
                    e.max(max8[:], logit_sb[:, sl])
                    e.drain()
                    e.tensor_scalar(gmask[:], logit_sb[:, sl], max8[:, 3:4], None, op0=ALU.is_ge)
                    inst = e.tensor_tensor(gates_sb[:, sl], logit_sb[:, sl], gmask[:], op=ALU.mult)
                    if i == NT_OWN - 1:
                        inst.then_inc(s_v.h, 1)
            em.on("vector", gates)
            gates_at = s_v.inc()

            if DEBUG_TAPS:
                def st_gates(e, gates_at=gates_at):
                    e.wait_ge(s_v.h, gates_at)
                    e.dma_start(out=gates_h[:], in_=gates_sb[:]).then_inc(s_out.h, 16)
                em.on("gpsimd", st_gates)
                s_out.inc(16)
            em.flush()

        p1_stores = s_out.n

        with ExitStack() as p2:
            sb2 = lambda n, shp: p2.enter_context(nc.sbuf_tensor(n, shp, F32))
            m1_sb = sb2("m1_sb", [128, 128])
            m2_sb = sb2("m2_sb", [128, 128])
            bqk_sb = sb2("bqk_sb", [128, 16])
            bv_sb = sb2("bv_sb", [128, DM])
            cosk_sb = sb2("cosk", [128, S])
            sink_sb = sb2("sink", [128, S])
            cosq_sb = sb2("cosq", [128, SOWN])
            sinq_sb = sb2("sinq", [128, SOWN])
            stream = [sb2(f"p2_st{i}", [128, 8 * 512]) for i in range(2)]
            wq_sb = sb2("p2_wq", [128, 8 * 128])
            wk_sb = sb2("p2_wk", [128, 8 * 128])
            wv_sb = sb2("p2_wv", [128, 8 * 128])
            krope = sb2("p2_krope", [128, S])
            kflip = sb2("p2_kflip", [128, S])
            qrope = sb2("p2_qrope", [128, SOWN])
            qflip = sb2("p2_qflip", [128, SOWN])
            vhp = sb2("p2_v", [128, NT_FULL * 130])
            absm = sb2("p2_absm", [128, 2])
            mtmp = sb2("p2_mtmp", [1, 16])
            negM = sb2("p2_negM", [128, 2])
            sq2 = p2.enter_context(nc.sbuf_tensor("p2_sq2", [128, S + SOWN], mybir.dt.bfloat16))
            ones128b = p2.enter_context(nc.sbuf_tensor("p2_ones128b", [128, 1], mybir.dt.bfloat16))
            psb = [[sb2(f"p2_p{h}{gbuf}", [128, 512]) for gbuf in range(2)] for h in range(2)]
            linv = sb2("p2_linv", [1, 512])
            linv2 = sb2("p2_linv2", [1, 512])
            linv_bc = [sb2(f"p2_linvbc{h}", [64, 512]) for h in range(2)]
            znT_sl = [sb2(f"p2_znTsl{h}", [64, 512]) for h in range(2)]

            def ld_tabs(e, p1s=p1_stores, ga=gates_at, sa=sig_at):
                e.wait_ge(s_out.h, p1s)
                e.wait_ge(s_v.h, ga)
                e.wait_ge(s_s.h, sa)
                for dst, src in ((cosk_sb, cos_k), (sink_sb, sin_k),
                                 (cosq_sb, cos_q), (sinq_sb, sin_q),
                                 (m1_sb, bmask[0]), (m2_sb, bmask[1]),
                                 (bqk_sb, bqk), (bv_sb, bv_rep)):
                    e.dma_start(out=dst[:], in_=src if src is not cos_k else src[:]).then_inc(s_in.h, 16)
            em.on("sync", ld_tabs)
            tabs_at = s_in.inc(128)

            def p2init(e):
                e.memset(ones128b[:], 1.0)
                e.memset(vhp.ap().rearrange("p (t x) -> p t x", t=NT_FULL)[:, :, 64:65], 1.0)
                e.memset(vhp.ap().rearrange("p (t x) -> p t x", t=NT_FULL)[:, :, 129:130], 1.0).then_inc(s_v.h, 1)
            em.on("vector", p2init)
            vones_at = s_v.inc()

            stream_free = [0, 0]
            hp_prev_v = 0
            av_buf_last = {0: 0, 1: 0}
            bankpair_free = {0: 0, 1: 0}
            zdiv_prev = 0
            bc_prev = 0
            av_done_last = 0
            znT_free = 0

            proj_last_t = 0
            for hp in range(HP):
                def ldw(e, hp=hp, pw=proj_last_t):
                    if pw:
                        e.wait_ge(s_t.h, pw)
                    for dst, srcw in ((wq_sb, wq), (wk_sb, wk), (wv_sb, wv)):
                        e.dma_start(out=dst.ap().rearrange("p (c m) -> p c m", c=8),
                                    in_=srcw.rearrange("(c p) m -> p c m", p=128)[:, :, hp * 128:hp * 128 + 128],
                                    ).then_inc(s_in.h, 16)
                em.on("sync", ldw)
                w_at = s_in.inc(48)

                prev_evac = 0
                flip_at = {}
                rope_at_sl = {}
                sq_at_sl = {}
                nred_prev = 0
                pend_nmm = None

                def emit_pend_nmm():
                    nonlocal pend_nmm
                    if pend_nmm is not None:
                        pend_nmm()
                        pend_nmm = None

                def emit_ropechain(si, buf, flp, costab, sintab, off, n):
                    nonlocal pend_nmm

                    def sqs(e, pd=prev_evac, buf=buf, si=si, off=off, n=n):
                        e.wait_ge(s_v.h, pd)
                        e.activation(out=sq2[:, si * 512:si * 512 + n], in_=buf[:, off:off + n],
                                     func=AF.Square).then_inc(s_s.h, 1)
                    em.on("scalar", sqs)
                    sq_at_sl[si] = s_s.inc()

                    def flips(e, pd=prev_evac, buf=buf, flp=flp, off=off, n=n):
                        e.wait_ge(s_v.h, pd)
                        for b0 in (0, 64):
                            e.dma_start(out=flp[b0:b0 + 32, off:off + n],
                                        in_=buf[b0 + 32:b0 + 64, off:off + n]).then_inc(s_out.h, 16)
                            e.dma_start(out=flp[b0 + 32:b0 + 64, off:off + n],
                                        in_=buf[b0:b0 + 32, off:off + n]).then_inc(s_out.h, 16)
                    em.on("gpsimd", flips)
                    flip_at[si] = s_out.inc(64)

                    def rope(e, fa=flip_at[si], sa=sq_at_sl[si], buf=buf, flp=flp,
                             costab=costab, sintab=sintab, off=off, n=n):
                        e.wait_ge(s_out.h, fa)
                        e.wait_ge(s_s.h, sa)
                        e.tensor_tensor(buf[:, off:off + n], buf[:, off:off + n],
                                        costab[:, off:off + n], op=ALU.mult)
                        e.tensor_tensor(flp[:, off:off + n], flp[:, off:off + n],
                                        sintab[:, off:off + n], op=ALU.mult)
                        e.tensor_tensor(buf[:, off:off + n], buf[:, off:off + n],
                                        flp[:, off:off + n], op=ALU.add).then_inc(s_v.h, 1)
                    em.on("vector", rope)
                    rope_at_sl[si] = s_v.inc()

                    def make_nmm(si=si, n=n, sa=sq_at_sl[si]):
                        def emit():
                            nonlocal nred_prev

                            def nmm(e, sa=sa, si=si, n=n, rp=nred_prev):
                                e.wait_ge(s_s.h, sa)
                                if rp:
                                    e.wait_ge(s_v.h, rp)
                                e.matmul(bank[7][0:1, 0:n], ones128b[:],
                                         sq2[:, si * 512:si * 512 + n],
                                         start=True, stop=True).then_inc(s_t.h, 1)
                            em.on("tensor", nmm)
                            nmm_at = s_t.inc()

                            def nred(e, na=nmm_at, si=si, n=n):
                                e.wait_ge(s_t.h, na)
                                e.tensor_reduce(mtmp[:, 2 + si:3 + si], bank[7][0:1, 0:n],
                                                axis=AX.X, op=ALU.max).then_inc(s_v.h, 1)
                            em.on("vector", nred)
                            nred_prev = s_v.inc()
                        return emit
                    pend_nmm = make_nmm()

                for sl_i in range(8):
                    ssl = sl_i % 2

                    def ldsl(e, ssl=ssl, sl_i=sl_i, lw=stream_free[ssl], hp=hp):
                        if hp == 0 and sl_i == 0:
                            e.wait_ge(s_out.h, p1_stores)
                        if lw:
                            e.wait_ge(s_t.h, lw)
                        e.dma_start(
                            out=stream[ssl].ap().rearrange("p (c t) -> p c t", c=8),
                            in_=xnT_full_h.rearrange("(c p) t -> p c t", p=128)[:, :, sl_i * 512:sl_i * 512 + 512],
                        ).then_inc(s_in.h, 16)
                    em.on("sync", ldsl)
                    sl_at = s_in.inc(16)

                    def mmkv(e, ssl=ssl, sl_at=sl_at, w_at=w_at, pe=prev_evac,
                             pv=hp_prev_v, first=(sl_i == 0)):
                        e.wait_ge(s_in.h, sl_at)
                        if first:
                            e.wait_ge(s_in.h, w_at)
                            e.wait_ge(s_in.h, tabs_at)
                            if pv:
                                e.wait_ge(s_v.h, pv)
                        if pe:
                            e.wait_ge(s_v.h, pe)
                        for c in range(8):
                            inst = e.matmul(bank[0][:],
                                            wk_sb[:, c * 128:c * 128 + 128],
                                            stream[ssl][:, c * 512:(c + 1) * 512],
                                            start=(c == 0), stop=(c == 7))
                            if c == 7:
                                inst.then_inc(s_t.h, 1)
                        for t in range(4):
                            for c in range(8):
                                inst = e.matmul(bank[2 + t][:, 0:128],
                                                stream[ssl][:, c * 512 + t * 128: c * 512 + t * 128 + 128],
                                                wv_sb[:, c * 128:c * 128 + 128],
                                                start=(c == 0), stop=(c == 7))
                                if c == 7:
                                    inst.then_inc(s_t.h, 1)
                    em.on("tensor", mmkv)
                    k_at = s_t.inc()
                    v_at = [s_t.inc() for _ in range(4)]
                    stream_free[ssl] = v_at[-1]
                    emit_pend_nmm()

                    def evkv(e, sl_i=sl_i, hp=hp, k_at=k_at, v_at=tuple(v_at)):
                        e.wait_ge(s_t.h, k_at)
                        e.tensor_scalar(krope[:, sl_i * 512:(sl_i + 1) * 512], bank[0][:],
                                        bqk_sb[:, hp * 2 + 1: hp * 2 + 2], None, op0=ALU.add)
                        e.wait_ge(s_t.h, v_at[-1])
                        for t in range(4):
                            gt = sl_i * 4 + t
                            for h in range(2):
                                inst = e.tensor_tensor(
                                    vhp[:, gt * 130 + h * 65: gt * 130 + h * 65 + 64],
                                    bank[2 + t][:, h * 64:h * 64 + 64],
                                    bv_sb[:, hp * 128 + h * 64: hp * 128 + h * 64 + 64],
                                    op=ALU.add)
                                if t == 3 and h == 1:
                                    inst.then_inc(s_v.h, 1)
                    em.on("vector", evkv)
                    prev_evac = s_v.inc()
                    emit_ropechain(sl_i, krope, kflip, cosk_sb, sink_sb, sl_i * 512, 512)
                    em.flush()

                for sl_i in range(4):
                    ssl = sl_i % 2

                    def ldslq(e, ssl=ssl, sl_i=sl_i, lw=stream_free[ssl]):
                        if lw:
                            e.wait_ge(s_t.h, lw)
                        e.dma_start(
                            out=stream[ssl].ap().rearrange("p (c t) -> p c t", c=8),
                            in_=xnT_own_h.rearrange("(c p) t -> p c t", p=128)[:, :, sl_i * 512:sl_i * 512 + 512],
                        ).then_inc(s_in.h, 16)
                    em.on("sync", ldslq)
                    sl_at = s_in.inc(16)

                    def mmq(e, ssl=ssl, sl_at=sl_at, pe=prev_evac):
                        e.wait_ge(s_in.h, sl_at)
                        e.wait_ge(s_v.h, pe)
                        for c in range(8):
                            inst = e.matmul(bank[6][:],
                                            wq_sb[:, c * 128:c * 128 + 128],
                                            stream[ssl][:, c * 512:(c + 1) * 512],
                                            start=(c == 0), stop=(c == 7))
                            if c == 7:
                                inst.then_inc(s_t.h, 1)
                    em.on("tensor", mmq)
                    q_at = s_t.inc()
                    stream_free[ssl] = q_at
                    emit_pend_nmm()

                    def evq(e, sl_i=sl_i, hp=hp, q_at=q_at):
                        e.wait_ge(s_t.h, q_at)
                        e.tensor_scalar(qrope[:, sl_i * 512:(sl_i + 1) * 512], bank[6][:],
                                        bqk_sb[:, hp * 2: hp * 2 + 1], None, op0=ALU.add).then_inc(s_v.h, 1)
                    em.on("vector", evq)
                    prev_evac = s_v.inc()
                    emit_ropechain(8 + sl_i, qrope, qflip, cosq_sb, sinq_sb, sl_i * 512, 512)
                    em.flush()
                emit_pend_nmm()
                proj_last_t = s_t.n

                def mcomb(e, rp=nred_prev):
                    e.wait_ge(s_v.h, rp)
                    e.drain()
                    e.tensor_reduce(mtmp[:, 0:1], mtmp[:, 2:10], axis=AX.X, op=ALU.max)
                    e.tensor_reduce(mtmp[:, 1:2], mtmp[:, 10:14], axis=AX.X, op=ALU.max)
                    e.drain()
                    e.tensor_scalar(mtmp[:, 14:15], mtmp[:, 0:1], 1.0, mtmp[:, 1:2],
                                    op0=ALU.mult, op1=ALU.add)
                    e.drain()
                    e.tensor_scalar(mtmp[:, 15:16], mtmp[:, 14:15], -1.05 / 16.0, None,
                                    op0=ALU.mult).then_inc(s_v.h, 1)
                em.on("vector", mcomb)
                mrow_at = s_v.inc()

                def mbc(e, mrow_at=mrow_at):
                    e.wait_ge(s_v.h, mrow_at)
                    e.matmul(bank[7][:, 300:301], ones1_sb[:], mtmp[:, 15:16],
                             start=True, stop=True).then_inc(s_t.h, 1)
                em.on("tensor", mbc)
                mbc_at = s_t.inc()

                def mev(e, mbc_at=mbc_at):
                    e.wait_ge(s_t.h, mbc_at)
                    e.tensor_copy(negM[:, 0:1], bank[7][:, 300:301]).then_inc(s_v.h, 1)
                em.on("vector", mev)
                negM_at = s_v.inc()
                em.flush()

                pend_av = None
                pend_div = None

                for m in range(4):
                    nkp = 8 * m + 8
                    for t in range(nkp):
                        par = t % 2
                        sA = bank[par]
                        sB = bank[2 + par]
                        pb = (psb[0][par], psb[1][par])
                        qsl = slice(m * 512, m * 512 + 512)
                        tau = t - 8 * m

                        smin = max(0, -((1 - tau) // 2)) if tau >= 0 else 0

                        def sc(e, t=t, m=m, smin=smin, sA=sA, sB=sB,
                               bpw=bankpair_free[par], negM_at=negM_at,
                               vones_at=vones_at, first=(m == 0 and t == 0)):
                            if first:
                                e.wait_ge(s_v.h, negM_at)
                                e.wait_ge(s_v.h, vones_at)
                            if bpw:
                                e.wait_ge(s_s.h, bpw)
                            ksl = slice(t * 128, t * 128 + 128)
                            for h, sX in ((0, sA), (1, sB)):
                                p0, p1 = 64 * h, 64 * h + 64
                                for s in range(smin, 4):
                                    qsl = slice((4 * m + s) * 128, (4 * m + s) * 128 + 128)
                                    inst = e.matmul(sX[:, s * 128:s * 128 + 128],
                                                    krope[p0:p1, ksl], qrope[p0:p1, qsl],
                                                    start=True, stop=True)
                                if h == 1:
                                    inst.then_inc(s_t.h, 1)
                        em.on("tensor", sc)
                        sc_at = s_t.inc()

                        def ex(e, sc_at=sc_at, sA=sA, sB=sB, pb=pb, smin=smin,
                               pav=av_buf_last[par]):
                            e.wait_ge(s_t.h, sc_at)
                            if pav:
                                e.wait_ge(s_t.h, pav)
                            e.activation(out=pb[0][:, smin * 128:512], in_=sA[:, smin * 128:512],
                                         func=AF.Exp, scale=ATTN_SCALE, bias=negM[:, 0:1])
                            e.activation(out=pb[1][:, smin * 128:512], in_=sB[:, smin * 128:512],
                                         func=AF.Exp, scale=ATTN_SCALE, bias=negM[:, 0:1]).then_inc(s_s.h, 1)
                        em.on("scalar", ex)
                        exp_at = s_s.inc()
                        bankpair_free[par] = exp_at

                        vops = []
                        if tau >= 0:
                            if tau % 2 == 0 and tau >= 2:
                                vops.append(("zero", (tau - 2) // 2))
                            if tau % 2 == 1 and tau >= 3:
                                vops.append(("zero", (tau - 3) // 2))
                            if 0 <= tau <= 7:
                                vops.append(("mask", tau))
                        mask_at = None
                        if vops:
                            def mk(e, exat=exp_at, vops=tuple(vops), pb=pb):
                                e.wait_ge(s_s.h, exat)
                                last = None
                                for kind, a in vops:
                                    if kind == "zero":
                                        for h in range(2):
                                            last = e.memset(pb[h][:, a * 128:a * 128 + 128], 0.0)
                                    else:
                                        mm_ = m1_sb if a % 2 == 0 else m2_sb
                                        blk = a // 2
                                        for h in range(2):
                                            last = e.tensor_tensor(pb[h][:, blk * 128:blk * 128 + 128],
                                                                   pb[h][:, blk * 128:blk * 128 + 128],
                                                                   mm_[:], op=ALU.mult)
                                last.then_inc(s_v.h, 1)
                            em.on("vector", mk)
                            mask_at = s_v.inc()

                        if pend_div is not None and t >= 1:
                            pend_div()
                            pend_div = None
                        if pend_av is not None:
                            pend_av()
                            pend_av = None

                        def make_av(t=t, exat=exp_at, mask_at=mask_at, pb=pb,
                                    nkp=nkp, m=m):
                            def emit():
                                nonlocal av_done_last
                                zw = zdiv_prev

                                def av(e, t=t, exat=exat, mask_at=mask_at, pb=pb,
                                       nkp=nkp, zw=zw):
                                    e.wait_ge(s_s.h, exat)
                                    if mask_at:
                                        e.wait_ge(s_v.h, mask_at)
                                    if t == 0 and zw:
                                        e.wait_ge(s_v.h, zw)
                                    for h in range(2):
                                        inst = e.matmul(bank[4 + h][0:65, 0:512],
                                                        vhp[:, t * 130 + h * 65: t * 130 + h * 65 + 65],
                                                        pb[h][:],
                                                        start=(t == 0), stop=(t == nkp - 1))
                                        if h == 1:
                                            inst.then_inc(s_t.h, 1)
                                em.on("tensor", av)
                                avat = s_t.inc()
                                av_buf_last[t % 2] = avat
                                av_done_last = avat
                            return emit
                        pend_av = make_av()

                    def make_div(m=m, hp=hp):
                        def emit():
                            nonlocal zdiv_prev, bc_prev, znT_free
                            av_done = av_done_last

                            def zrec(e, av_done=av_done):
                                e.wait_ge(s_t.h, av_done)
                                e.reciprocal(linv[:], bank[4][64:65, 0:512])
                                e.reciprocal(linv2[:], bank[5][64:65, 0:512]).then_inc(s_v.h, 1)
                            em.on("vector", zrec)
                            rec_at = s_v.inc()

                            def zbc(e, rec_at=rec_at, bp=bc_prev):
                                e.wait_ge(s_v.h, rec_at)
                                if bp:
                                    e.wait_ge(s_v.h, bp)
                                e.matmul(bank[6][:, 0:512], ones1_sb[:], linv[:],
                                         start=True, stop=True)
                                e.matmul(bank[7][:, 0:512], ones1_sb[:], linv2[:],
                                         start=True, stop=True).then_inc(s_t.h, 1)
                            em.on("tensor", zbc)
                            zbc_at = s_t.inc()

                            def zmul(e, zbc_at=zbc_at, zf=znT_free):
                                e.wait_ge(s_t.h, zbc_at)
                                if zf:
                                    e.wait_ge(s_out.h, zf)
                                e.tensor_copy(linv_bc[0][:], bank[6][0:64, 0:512])
                                e.tensor_copy(linv_bc[1][:], bank[7][0:64, 0:512])
                                e.tensor_tensor(znT_sl[0][:], bank[4][0:64, 0:512],
                                                linv_bc[0][:], op=ALU.mult)
                                e.tensor_tensor(znT_sl[1][:], bank[5][0:64, 0:512],
                                                linv_bc[1][:], op=ALU.mult).then_inc(s_v.h, 1)
                            em.on("vector", zmul)
                            zdiv_prev = s_v.inc()
                            bc_prev = zdiv_prev

                            def zst(e, za=zdiv_prev, hp=hp, m=m):
                                e.wait_ge(s_v.h, za)
                                for h in range(2):
                                    e.dma_start(out=znT_h[hp * 128 + h * 64: hp * 128 + h * 64 + 64,
                                                          m * 512:m * 512 + 512],
                                                in_=znT_sl[h][:]).then_inc(s_out.h, 16)
                            em.on("gpsimd", zst)
                            znT_free = s_out.inc(32)
                        return emit
                    pend_div = make_div()
                    em.flush()

                if pend_av is not None:
                    pend_av()
                    pend_av = None
                if pend_div is not None:
                    pend_div()
                    pend_div = None
                em.flush()

                hp_prev_v = s_v.n
            zn_stores = s_out.n
            p2_final_v = s_v.n

        p35 = ctx.enter_context(ExitStack())
        acc_sb = p35.enter_context(nc.sbuf_tensor("acc_sb", [128, NT_OWN * DM], F32))

        with ExitStack() as p3:
            sb3 = lambda n, shp: p3.enter_context(nc.sbuf_tensor(n, shp, F32))
            bo_sb = sb3("p3_bo", [128, DM])
            znT_sb = sb3("p3_znT", [128, 8 * SOWN])
            wo_sb = sb3("p3_wo", [128, 8 * DM])
            xs = [sb3(f"p3_x{i}", [128, DM]) for i in range(2)]

            def p3ld(e, zn_stores=zn_stores, pv=p2_final_v):
                e.wait_ge(s_out.h, zn_stores)
                e.wait_ge(s_v.h, pv)
                e.dma_start(out=znT_sb.ap().rearrange("p (c t) -> p c t", c=8),
                            in_=znT_h.rearrange("(c p) t -> p c t", p=128)).then_inc(s_in.h, 16)
                e.dma_start(out=wo_sb.ap().rearrange("p (c m) -> p c m", c=8),
                            in_=wo.rearrange("(c p) m -> p c m", p=128)).then_inc(s_in.h, 16)
                e.dma_start(out=bo_sb[:], in_=bo_rep[:]).then_inc(s_in.h, 16)
            em.on("sync", p3ld)
            p3_at = s_in.inc(48)

            x_free = [0, 0]
            o_evo_hist = {}
            for i in range(NT_OWN):
                slot = i % 2

                def ldx(e, i=i, slot=slot, lw=x_free[slot], p3_at=p3_at):
                    if lw:
                        e.wait_ge(s_v.h, lw)
                    e.dma_start(out=xs[slot][:], in_=x_own[i * 128:(i + 1) * 128, :]).then_inc(s_in.h, 16)
                em.on("sync", ldx)
                x_at = s_in.inc(16)

                def mmo(e, i=i, p3_at=p3_at, pv=p2_final_v, bw=o_evo_hist.get(i - 2)):
                    if i == 0:
                        e.wait_ge(s_in.h, p3_at)
                        e.wait_ge(s_v.h, pv)
                    if bw:
                        e.wait_ge(s_v.h, bw)
                    for n in range(2):
                        for c in range(8):
                            inst = e.matmul(bank[2 * (i % 2) + n][:],
                                            znT_sb[:, c * 2048 + i * 128: c * 2048 + i * 128 + 128],
                                            wo_sb[:, c * 1024 + n * 512: c * 1024 + n * 512 + 512],
                                            start=(c == 0), stop=(c == 7))
                            if c == 7:
                                inst.then_inc(s_t.h, 1)
                em.on("tensor", mmo)
                o_at = [s_t.inc(), s_t.inc()]

                def evo(e, i=i, slot=slot, x_at=x_at, o_at=tuple(o_at)):
                    e.wait_ge(s_in.h, x_at)
                    for n in range(2):
                        e.wait_ge(s_t.h, o_at[n])
                        dsl = slice(i * DM + n * 512, i * DM + n * 512 + 512)
                        e.tensor_tensor(acc_sb[:, dsl], bank[2 * (i % 2) + n][:],
                                        bo_sb[:, n * 512:n * 512 + 512], op=ALU.add)
                        inst = e.tensor_tensor(acc_sb[:, dsl], acc_sb[:, dsl],
                                               xs[slot][:, n * 512:n * 512 + 512], op=ALU.add)
                    inst.then_inc(s_v.h, 1)
                em.on("vector", evo)
                x_free[slot] = s_v.inc()
                o_evo_hist[i] = x_free[slot]
                if i % 4 == 3:
                    em.flush()
            em.flush()
            acc_seeded = s_v.n

        with ExitStack() as p5:
            sb5 = lambda n, shp: p5.enter_context(nc.sbuf_tensor(n, shp, F32))
            xnTo = sb5("p5_xnTo", [128, 8 * SOWN])
            ks = [sb5(f"p5_k{i}", [128, 8 * 512]) for i in range(2)]
            vs = [sb5(f"p5_v{i}", [128, 4 * DM]) for i in range(2)]
            hg = sb5("p5_hg", [128, 512])
            hgT = sb5("p5_hgT", [128, 512])

            def p5ld0(e, ac=acc_seeded):
                e.wait_ge(s_v.h, ac)
                e.dma_start(out=xnTo.ap().rearrange("p (c t) -> p c t", c=8),
                            in_=xnT_own_h.rearrange("(c p) t -> p c t", p=128)).then_inc(s_in.h, 16)
            em.on("sync", p5ld0)
            xnTo_at = s_in.inc(16)

            ks_free = [0, 0]
            store_after = [0] * NT_OWN
            gates_dep = gates_at
            hg_at_hist = {}
            tp_at_hist = {}

            for u in range(NUB):
                su = u % 2

                def ldu(e, u=u, su=su, kw=ks_free[su]):
                    if kw:
                        e.wait_ge(s_t.h, kw)
                    e.dma_start(out=ks[su].ap().rearrange("p (c u) -> p c u", c=8),
                                in_=keys_l.rearrange("(c p) u -> p c u", p=128)[:, :, u * 512:u * 512 + 512],
                                ).then_inc(s_in.h, 16)
                    e.dma_start(out=vs[su].ap().rearrange("p (s m) -> p s m", s=4),
                                in_=values_l.rearrange("(s p) m -> p s m", p=128)[:, u * 4:u * 4 + 4, :],
                                ).then_inc(s_in.h, 16)
                em.on("sync", ldu)
                k_at = s_in.inc(16)
                v_at = s_in.inc(16)

                pend5 = None
                for t in range(NT_OWN + 1):
                    if t < NT_OWN:
                        hb = bank[t % 2]

                        def mmh(e, u=u, su=su, t=t, hb=hb, k_at=k_at, xnTo_at=xnTo_at,
                                bw=hg_at_hist.get(t - 2), acc_seeded=acc_seeded):
                            if u == 0 and t == 0:
                                e.wait_ge(s_in.h, xnTo_at)
                                e.wait_ge(s_v.h, acc_seeded)
                            if t == 0:
                                e.wait_ge(s_in.h, k_at)
                            if bw:
                                e.wait_ge(s_v.h, bw)
                            for c in range(8):
                                inst = e.matmul(hb[:], xnTo[:, c * 2048 + t * 128: c * 2048 + t * 128 + 128],
                                                ks[su][:, c * 512:(c + 1) * 512],
                                                start=(c == 0), stop=(c == 7))
                                if c == 7:
                                    inst.then_inc(s_t.h, 1)
                        em.on("tensor", mmh)
                        h_at = s_t.inc()

                    if pend5 is not None:
                        pend5()
                        pend5 = None

                    if t < NT_OWN:
                        def evh(e, u=u, t=t, hb=hb, h_at=h_at, gdep=gates_dep,
                                tpw=tp_at_hist.get(t - 1)):
                            if u == 0 and t == 0 and gdep:
                                e.wait_ge(s_v.h, gdep)
                            e.wait_ge(s_t.h, h_at)
                            if tpw:
                                e.wait_ge(s_t.h, tpw)
                            for i in range(4):
                                eidx = u * 4 + i
                                inst = e.tensor_scalar(hg[:, i * 128:(i + 1) * 128], hb[:, i * 128:(i + 1) * 128],
                                                       gates_sb[:, t * 64 + eidx: t * 64 + eidx + 1], 0.0,
                                                       op0=ALU.mult, op1=ALU.max)
                            inst.then_inc(s_v.h, 1)
                        em.on("vector", evh)
                        hg_at_hist[t] = s_v.inc()

                        def make_deferred(u=u, su=su, t=t, v_at=v_at, hg_at=hg_at_hist[t]):
                            def emit():
                                tb = bank[2 + t % 2]

                                def tph(e, tb=tb, hg_at=hg_at):
                                    e.wait_ge(s_v.h, hg_at)
                                    for i in range(4):
                                        inst = e.transpose(tb[:, i * 128:(i + 1) * 128],
                                                           hg[:, i * 128:(i + 1) * 128], ident_sb[:])
                                        if i == 3:
                                            inst.then_inc(s_t.h, 1)
                                em.on("tensor", tph)
                                tp_at = s_t.inc()
                                tp_at_hist[t] = tp_at

                                def evT(e, tb=tb, tp_at=tp_at):
                                    e.wait_ge(s_t.h, tp_at)
                                    e.tensor_copy(hgT[:], tb[:]).then_inc(s_v.h, 1)
                                em.on("vector", evT)
                                T_at = s_v.inc()

                                def mm2(e, su=su, t=t, v_at=v_at, T_at=T_at):
                                    if t == 0:
                                        e.wait_ge(s_in.h, v_at)
                                    e.wait_ge(s_v.h, T_at)
                                    for n in range(2):
                                        for s4 in range(4):
                                            inst = e.matmul(bank[4 + 2 * (t % 2) + n][:],
                                                            hgT[:, s4 * 128:(s4 + 1) * 128],
                                                            vs[su][:, s4 * 1024 + n * 512: s4 * 1024 + n * 512 + 512],
                                                            start=(s4 == 0), stop=(s4 == 3))
                                            if s4 == 3:
                                                inst.then_inc(s_t.h, 1)
                                em.on("tensor", mm2)
                                f_at = [s_t.inc(), s_t.inc()]

                                def addacc(e, t=t, f_at=tuple(f_at)):
                                    for n in range(2):
                                        e.wait_ge(s_t.h, f_at[n])
                                        dsl = slice(t * DM + n * 512, t * DM + n * 512 + 512)
                                        inst = e.scalar_tensor_tensor(acc_sb[:, dsl],
                                                                      bank[4 + 2 * (t % 2) + n][:], 1.0,
                                                                      acc_sb[:, dsl], op0=ALU.mult, op1=ALU.add)
                                    inst.then_inc(s_v.h, 1)
                                em.on("vector", addacc)
                                add_at = s_v.inc()
                                if u == NUB - 1:
                                    store_after[t] = add_at
                            return emit
                        pend5 = make_deferred()
                    if t % 4 == 3:
                        em.flush()
                ks_free[su] = s_t.n
                em.flush()

            def stout(e, store_after=tuple(store_after), base=s_out.n):
                for t in range(NT_OWN):
                    e.wait_ge(s_v.h, store_after[t])
                    e.dma_start(out=out[t * 128:(t + 1) * 128, :],
                                in_=acc_sb[:, t * DM:(t + 1) * DM]).then_inc(s_out.h, 16)
                e.wait_ge(s_out.h, base + 16 * NT_OWN)
            em.on("gpsimd", stout)
            s_out.inc(16 * NT_OWN)
            em.flush()

    return nc


def make_inmaps(inputs):
    x = np.asarray(inputs["L0_attn_x"], dtype=np.float32)
    WQ = np.asarray(inputs["L0_attn_W_Q"], dtype=np.float32)
    WK = np.asarray(inputs["L0_attn_W_K"], dtype=np.float32)
    WV = np.asarray(inputs["L0_attn_W_V"], dtype=np.float32)
    WO = np.asarray(inputs["L0_attn_W_O"], dtype=np.float32)
    bQ = np.asarray(inputs["L0_attn_b_Q"], dtype=np.float32)
    bK = np.asarray(inputs["L0_attn_b_K"], dtype=np.float32)
    bV = np.asarray(inputs["L0_attn_b_V"], dtype=np.float32)
    bO = np.asarray(inputs["L0_attn_b_O"], dtype=np.float32)
    esel = np.asarray(inputs["L0_ffn_expert_sel"], dtype=np.float32)
    keys = np.asarray(inputs["L0_ffn_keys"], dtype=np.float32)
    values = np.asarray(inputs["L0_ffn_values"], dtype=np.float32)

    wq_l = np.ascontiguousarray(WQ.transpose(1, 0, 2).reshape(DM, DM))
    wk_l = np.ascontiguousarray(WK.transpose(1, 0, 2).reshape(DM, DM))
    wv_l = np.ascontiguousarray(WV.transpose(1, 0, 2).reshape(DM, DM))
    wo_l = np.ascontiguousarray(WO.reshape(DM, DM))
    selT = np.ascontiguousarray(esel.T)
    keys_l = np.ascontiguousarray(keys.transpose(1, 0, 2).reshape(DM, UDIM))
    values_l = np.ascontiguousarray(values.reshape(UDIM, DM))

    bqk = np.zeros((128, 16), np.float32)
    for hp in range(HP):
        for h in range(2):
            bqk[h * 64:(h + 1) * 64, hp * 2 + 0] = bQ[2 * hp + h]
            bqk[h * 64:(h + 1) * 64, hp * 2 + 1] = bK[2 * hp + h]
    bv_rep = np.ascontiguousarray(np.broadcast_to(bV.reshape(1, DM), (128, DM)))
    bo_rep = np.ascontiguousarray(np.broadcast_to(bO.reshape(1, DM), (128, DM)))

    pos = np.arange(S, dtype=np.float64)
    jj = np.arange(32, dtype=np.float64)
    freq = 10000.0 ** (jj / 32.0)
    ang = pos[:, None] / freq[None, :]
    cos32 = np.cos(ang)
    sin32 = np.sin(ang)
    cosh = np.concatenate([cos32, cos32], axis=1).T
    sinh = np.concatenate([-sin32, sin32], axis=1).T
    cos_k = np.ascontiguousarray(np.concatenate([cosh, cosh], 0), np.float32) \
        .astype(np.float32)
    sin_k = np.ascontiguousarray(np.concatenate([sinh, sinh], 0)).astype(np.float32)

    identm = np.eye(128, dtype=np.float32)
    tri = np.triu(np.ones((128, 128), np.float32))
    onesm = np.ones((128, 128), np.float32)
    zerosm = np.zeros((128, 128), np.float32)

    in_maps = []
    own_rows_all = []
    for core in range(8):
        b, r = core // 2, core % 2
        gs = [2 * j + (1 - r) for j in range(16)]
        own_rows = np.concatenate([np.arange(128 * g, 128 * g + 128) for g in gs])
        own_rows_all.append(own_rows)
        x_b = np.ascontiguousarray(x[b])
        x_ownc = np.ascontiguousarray(x_b[own_rows])
        cos_q = np.ascontiguousarray(cos_k[:, own_rows])
        sin_q = np.ascontiguousarray(sin_k[:, own_rows])
        bmask = np.stack([onesm, tri] if r == 0 else [tri, zerosm])
        in_maps.append({
            "x_full": x_b, "x_own": x_ownc,
            "wq": wq_l, "wk": wk_l, "wv": wv_l, "wo": wo_l,
            "bqk": bqk, "bv_rep": bv_rep, "bo_rep": bo_rep,
            "selT": selT, "keys_l": keys_l, "values_l": values_l,
            "cos_k": cos_k, "sin_k": sin_k, "cos_q": cos_q, "sin_q": sin_q,
            "ident": identm, "bmask": np.ascontiguousarray(bmask),
        })
    return in_maps, own_rows_all


def kernel(**inputs):
    in_maps, own_rows_all = make_inmaps(inputs)
    nc = build_program()
    res = run_bass_kernel_spmd(nc, in_maps, core_ids=list(range(8)))
    out = np.empty((B, S, DM), dtype=np.float32)
    for core in range(8):
        b = core // 2
        out[b].reshape(S, DM)[own_rows_all[core]] = res.results[core]["out"]
    return out
